# version 15
# baseline (speedup 1.0000x reference)
import numpy as np
import ml_dtypes
from contextlib import ExitStack

import concourse.bass as bass
import concourse.mybir as mybir
from concourse.bass_utils import run_bass_kernel_spmd

F32 = mybir.dt.float32
BF16 = mybir.dt.bfloat16
U32 = mybir.dt.uint32
AF = mybir.ActivationFunctionType
ALU = mybir.AluOpType
AX = mybir.AxisListType.X
RG = [[0, 1, 2, 3, 4, 5, 6, 7]]

D = 2048
N = 8192
E = 8192
TOK = 1024
ISQ = float(1.0 / np.sqrt(128.0))
EPS_LN = 1e-5
EPS_Q = 1e-8
NEL_W = float(E * D)

DEBUG = False
HOIST_EH = 3


class Hist:
    def __init__(self, depth):
        self.depth = depth
        self.marks = []

    def guard(self):
        if len(self.marks) >= self.depth:
            return self.marks[-self.depth]
        return None

    def push(self, sem_name, value):
        self.marks.append((sem_name, value))


def _apkey(a):
    try:
        return (a.memref, a.offset, tuple(tuple(p) for p in a.ap), a.dtype)
    except Exception:
        return None


def _clone_noldw(inst, name_suffix="_noldw"):
    kw = dict(name=inst.name + name_suffix, ins=list(inst.ins), outs=list(inst.outs),
              start_tensor_calc=inst.start_tensor_calc,
              stop_tensor_calc=inst.stop_tensor_calc,
              replication_resolution=inst.replication_resolution,
              replication_shift_amnt=inst.replication_shift_amnt,
              replication_num_rows=inst.replication_num_rows,
              ldweights=False)
    for f in ("perf_mode", "is_transpose", "ifmap_quant_offset",
              "weights_quant_offset", "tile_position", "tile_size"):
        v = getattr(inst, f)
        if v is not None:
            kw[f] = v
    n = mybir.InstMatmult(**kw)
    n.sync_info = inst.sync_info
    n.engine = inst.engine
    return n


def dedupe_ldweights(nc):
    nrw = 0
    for b in nc.m.functions[0].blocks:
        prev_key = None
        insts = b.instructions
        for i in range(len(insts)):
            inst = insts[i]
            cn = inst.__class__.__name__
            if cn == "InstMatmult":
                if inst.is_transpose:
                    prev_key = None
                    continue
                k = _apkey(inst.ins[1])
                if (k is not None and k == prev_key
                        and inst.ins[1].dtype == BF16):
                    newi = _clone_noldw(inst)
                    nc.register_instruction(newi, overwrite=True)
                    insts[i] = newi
                    nrw += 1
                else:
                    prev_key = k
            elif getattr(inst, "is_sequencer_only", False):
                continue
            else:
                prev_key = None
    return nrw


def build_program():
    nc = bass.Bass("TRN2", target_bir_lowering=False, detect_race_conditions=False)

    xs = nc.dram_tensor("xs", [TOK, D], F32, kind="ExternalInput")
    wq = nc.dram_tensor("wq", [256, D], F32, kind="ExternalInput")
    wk = nc.dram_tensor("wk", [256, D], F32, kind="ExternalInput")
    wv = nc.dram_tensor("wv", [256, D], F32, kind="ExternalInput")
    wo = nc.dram_tensor("wo", [256, D], F32, kind="ExternalInput")
    bq = nc.dram_tensor("bq", [256, 1], F32, kind="ExternalInput")
    bk = nc.dram_tensor("bk", [256, 1], F32, kind="ExternalInput")
    bv = nc.dram_tensor("bv", [256, 1], F32, kind="ExternalInput")
    bo = nc.dram_tensor("bo", [1, D], F32, kind="ExternalInput")
    wu = nc.dram_tensor("wu", [1024, D], F32, kind="ExternalInput")
    wgt = nc.dram_tensor("wgt", [1024, D], F32, kind="ExternalInput")
    wd = nc.dram_tensor("wd", [256, E], F32, kind="ExternalInput")
    out = nc.dram_tensor("out", [TOK, D], F32, kind="ExternalOutput")
    pid_t = nc.partition_id_tensor

    xnh_in = nc.dram_tensor("xnh_in", [TOK, D], BF16, kind="Internal")
    xnl_in = nc.dram_tensor("xnl_in", [TOK, D], BF16, kind="Internal")
    xnhA = nc.dram_tensor("xnhA", [N // 2, D], BF16, kind="Internal")
    xnlA = nc.dram_tensor("xnlA", [N // 2, D], BF16, kind="Internal")
    xnhB = nc.dram_tensor("xnhB", [N // 2, D], BF16, kind="Internal")
    xnlB = nc.dram_tensor("xnlB", [N // 2, D], BF16, kind="Internal")
    wsp = {}
    for nm in ("wqh", "wql", "wkh", "wkl", "wvh", "wvl", "woh_in", "wol_in"):
        wsp[nm] = nc.dram_tensor(nm, [256, D], BF16, kind="Internal")
    woh = nc.dram_tensor("woh", [D, D], BF16, kind="Internal")
    wol = nc.dram_tensor("wol", [D, D], BF16, kind="Internal")
    scp_in = nc.dram_tensor("scp_in", [1, 16], F32, kind="Internal")
    scp = nc.dram_tensor("scp", [8, 16], F32, kind="Internal")
    scw = nc.dram_tensor("scw", [1, 16], F32, kind="Internal")
    tu_in = nc.dram_tensor("tu_in", [1024, D], BF16, kind="Internal")
    tg_in = nc.dram_tensor("tg_in", [1024, D], BF16, kind="Internal")
    td_in = nc.dram_tensor("td_in", [256, E], BF16, kind="Internal")
    tu = nc.dram_tensor("tu", [E, D], BF16, kind="Internal")
    tgg = nc.dram_tensor("tgg", [E, D], BF16, kind="Internal")
    td = nc.dram_tensor("td", [D, E], BF16, kind="Internal")
    ctxh_in = nc.dram_tensor("ctxh_in", [N, 256], BF16, kind="Internal")
    ctxl_in = nc.dram_tensor("ctxl_in", [N, 256], BF16, kind="Internal")
    ctxh = nc.dram_tensor("ctxh", [8 * N, 256], BF16, kind="Internal")
    ctxl = nc.dram_tensor("ctxl", [8 * N, 256], BF16, kind="Internal")
    myctxh = nc.dram_tensor("myctxh", [TOK, D], BF16, kind="Internal")
    myctxl = nc.dram_tensor("myctxl", [TOK, D], BF16, kind="Internal")
    yff = nc.dram_tensor("yff", [TOK, D], F32, kind="Internal")

    dbg = {}
    if DEBUG:
        for nm, shp, dt in [("dbg_xnh", [TOK, D], BF16), ("dbg_ctxh", [N, 256], BF16),
                            ("dbg_yff", [TOK, D], F32), ("dbg_scp", [8, 16], F32),
                            ("dbg_tu", [1024, D], BF16)]:
            dbg[nm] = nc.dram_tensor(nm, shp, dt, kind="ExternalOutput")

    ops = {k: [] for k in "tvsyg"}
    cnt = {k: 0 for k in ("PT", "PV", "PS", "PY", "PG")}
    sems = {}
    DMA_STREAMS = ("yc", "gpid", "x2a", "x2b", "hl0", "hl1", "gsc", "qsl", "qsw",
                   "qbc", "ywt", "xt0", "xt1", "ctx0", "ctx1", "grp",
                   "yct", "yctb", "yctc", "yctd",
                   "yfx", "yfxb", "yfxc", "yfxd", "td0", "td1",
                   "qw0", "qw1", "qts0", "qts1", "yf0", "yf1", "xt2", "xt3",
                   "td2", "td3", "td4", "td5",
                   "tu00", "tu01", "tu02", "tu03", "tu10", "tu11", "tu12", "tu13",
                   "tu20", "tu21", "tu22", "tu23",
                   "wo00", "wo01", "wo02", "wo03", "wo10", "wo11", "wo12", "wo13",
                   "xr0", "xr1", "ou0", "ou1")
    CC_STREAMS = ("ccxh", "ccxl", "ccxh2", "ccxl2", "ccwh", "ccwl", "ccsc",
                  "cctu", "cctg", "cctd", "ccch", "cccl")
    for k in DMA_STREAMS + CC_STREAMS:
        cnt[k] = 0

    def emit(e, fn):
        ops[e].append(fn)

    def inc(name, k=1):
        cnt[name] += k
        return cnt[name]

    with ExitStack() as ctx:
        for name in cnt:
            sems[name] = ctx.enter_context(nc.semaphore(name))
        S = sems

        def barrier():
            pm = {"t": "PT", "v": "PV", "s": "PS", "y": "PY", "g": "PG"}
            tg = {}
            for e in "tvsyg":
                tg[pm[e]] = inc(pm[e])
            snap = dict(tg)
            for e in "tvsyg":
                def f(eng, e=e, snap=snap, pm=pm):
                    eng.drain().then_inc(S[pm[e]], 1)
                    for o in "tvsyg":
                        if o != e:
                            eng.wait_ge(S[pm[o]], snap[pm[o]])
                emit(e, f)

        s_bqs = ctx.enter_context(nc.sbuf_tensor("s_bqs", [128, 2], F32))
        s_bks = ctx.enter_context(nc.sbuf_tensor("s_bks", [128, 2], F32))
        s_bvs = ctx.enter_context(nc.sbuf_tensor("s_bvs", [128, 2], F32))
        s_bob = ctx.enter_context(nc.sbuf_tensor("s_bob", [128, D], F32))
        s_mask = ctx.enter_context(nc.sbuf_tensor("s_mask", [128, 4 * 512], F32))
        s_ident = ctx.enter_context(nc.sbuf_tensor("s_ident", [128, 128], F32))
        s_ones_c = ctx.enter_context(nc.sbuf_tensor("s_ones_c", [128, 1], F32))
        s_ones_cb = ctx.enter_context(nc.sbuf_tensor("s_ones_cb", [128, 1], BF16))
        s_ones_r = ctx.enter_context(nc.sbuf_tensor("s_ones_r", [1, 128], F32))
        s_pid = ctx.enter_context(nc.sbuf_tensor("s_pid", [1, 1], U32))
        s_scales = ctx.enter_context(nc.sbuf_tensor("s_scales", [128, 3], F32))
        s_thr = ctx.enter_context(nc.sbuf_tensor("s_thr", [128, 6], F32))
        s_negmh = ctx.enter_context(nc.sbuf_tensor("s_negmh", [128, 1], F32))

        pidr = nc.gpsimd.alloc_register("pidr")
        pidoff = nc.gpsimd.alloc_register("pidoff")
        tmpr = nc.gpsimd.alloc_register("tmpr")

        pq = ExitStack()
        s_q2 = [pq.enter_context(nc.sbuf_tensor("s_q2_0", [128, D], F32))] * 2
        s_qq = pq.enter_context(nc.sbuf_tensor("s_qq", [128, D], BF16))
        s_qt = [pq.enter_context(nc.sbuf_tensor("s_qt_0", [128, D], BF16))] * 2
        s_qacc = pq.enter_context(nc.sbuf_tensor("s_qacc", [128, 4], F32))
        s_qabs = pq.enter_context(nc.sbuf_tensor("s_qabs", [128, 1], F32))
        s_qsc = pq.enter_context(nc.sbuf_tensor("s_qsc", [1, 16], F32))
        s_qall = pq.enter_context(nc.sbuf_tensor("s_qall", [1, 128], F32))

        p0 = ExitStack()
        s_x2 = [p0.enter_context(nc.sbuf_tensor(f"s_x2_{i}", [128, D], F32)) for i in range(2)]
        s_xn = p0.enter_context(nc.sbuf_tensor("s_xn", [128, D], F32))
        s_h2 = [p0.enter_context(nc.sbuf_tensor(f"s_h2_{i}", [128, D], BF16)) for i in range(2)]
        s_l2 = [p0.enter_context(nc.sbuf_tensor(f"s_l2_{i}", [128, D], BF16)) for i in range(2)]
        s_st = p0.enter_context(nc.sbuf_tensor("s_st", [128, 48], F32))
        s_sq0 = p0.enter_context(nc.sbuf_tensor("s_sq0", [128, D], F32))

        hx2 = Hist(2)
        hhl = Hist(2)
        hq2 = Hist(1)
        hqt = Hist(1)

        def f(y):
            y.dma_start(s_bqs[:], bass.AP(bq, 0, [[1, 128], [128, 2]]),
                        allow_slow_non_contiguous=True).then_inc(S["yc"], 16)
            y.dma_start(s_bks[:], bass.AP(bk, 0, [[1, 128], [128, 2]]),
                        allow_slow_non_contiguous=True).then_inc(S["yc"], 16)
            y.dma_start(s_bvs[:], bass.AP(bv, 0, [[1, 128], [128, 2]]),
                        allow_slow_non_contiguous=True).then_inc(S["yc"], 16)
            y.dma_start(s_bob[:], bass.AP(bo, 0, [[0, 128], [1, D]])).then_inc(S["yc"], 16)
        emit("y", f)
        CONST_LD = inc("yc", 64)

        def f(g):
            g.dma_start(s_pid[:], pid_t[:, :]).then_inc(S["gpid"], 16)
        emit("g", f)
        PID_LD = inc("gpid", 16)

        def f(g, PID_LD=PID_LD):
            g.wait_ge(S["gpid"], PID_LD)
            g.reg_load(pidr, s_pid[:1, :1])
            g.reg_mul(pidoff, pidr, TOK * 256)
            g.drain().then_inc(S["PG"], 1)
        emit("g", f)
        PIDREG_DONE = inc("PG")

        def f(v):
            for dlt in range(4):
                v.memset(s_mask[:, dlt * 512:(dlt + 1) * 512], 1.0)
            v.memset(s_ident[:], 1.0)
            v.drain().then_inc(S["PV"], 1)
        emit("v", f)
        MEMSET_DONE = inc("PV")

        def f(g, MEMSET_DONE=MEMSET_DONE):
            g.wait_ge(S["PV"], MEMSET_DONE)
            for dlt in range(4):
                g.affine_select(s_mask[:, dlt * 512:(dlt + 1) * 512],
                                s_mask[:, dlt * 512:(dlt + 1) * 512],
                                [[1, 512]], ALU.is_ge, 0.0,
                                base=-(128 * dlt),
                                channel_multiplier=-1).then_inc(S["PG"], 1)
            g.affine_select(s_ident[:], s_ident[:], [[1, 128]], ALU.is_equal, 0.0,
                            base=0, channel_multiplier=-1).then_inc(S["PG"], 1)
        emit("g", f)
        MASKS_DONE = inc("PG", 5)

        def f(v, CONST_LD=CONST_LD):
            v.memset(s_ones_c[:], 1.0)
            v.memset(s_ones_cb[:], 1.0)
            v.memset(s_ones_r[:], 1.0)
            v.memset(s_qacc[:], 0.0)
            v.memset(s_qsc[:], 0.0)
            v.wait_ge(S["yc"], CONST_LD)
            v.tensor_scalar_mul(s_bqs[:], s_bqs[:], ISQ)
            v.drain().then_inc(S["PV"], 1)
        emit("v", f)
        INIT_V = inc("PV")

        for t in range(8):
            gd = hx2.guard()
            def f(y, t=t, gd=gd):
                if gd:
                    y.wait_ge(S[gd[0]], gd[1])
                sl = "x2a" if t % 2 == 0 else "x2b"
                y.dma_start(s_x2[t % 2][:], xs[t * 128:(t + 1) * 128, :]).then_inc(S[sl], 16)
            emit("y", f)
            sl = "x2a" if t % 2 == 0 else "x2b"
            LDX = inc(sl, 16)

            def f(v, t=t, LDX=LDX, sl=sl):
                v.wait_ge(S[sl], LDX)
                xt = s_x2[t % 2]
                mu = s_st[:, t:t + 1]; ss = s_st[:, 8 + t:9 + t]
                var = s_st[:, 16 + t:17 + t]; ms = s_st[:, 24 + t:25 + t]
                v.reduce_sum(mu, xt[:], axis=AX)
                v.tensor_mul(s_sq0[:], xt[:], xt[:])
                v.drain()
                v.reduce_sum(ss, s_sq0[:], axis=AX)
                v.tensor_scalar_mul(mu, mu, 1.0 / D)
                v.drain()
                v.tensor_mul(ms, mu, mu)
                v.drain()
                v.scalar_tensor_tensor(var, ss, 1.0 / D, ms, ALU.mult, ALU.subtract)
                v.drain()
                v.tensor_scalar_add(var, var, EPS_LN)
                v.drain().then_inc(S["PV"], 1)
            emit("v", f)
            VARD = inc("PV")

            def f(s, t=t, VARD=VARD):
                s.wait_ge(S["PV"], VARD)
                s.activation(s_st[:, 24 + t:25 + t], s_st[:, 16 + t:17 + t], AF.Sqrt)
                s.drain().then_inc(S["PS"], 1)
            emit("s", f)
            SDD = inc("PS")

            gd = hhl.guard()
            def f(v, t=t, SDD=SDD, gd=gd):
                v.wait_ge(S["PS"], SDD)
                if gd:
                    v.wait_ge(S[gd[0]], gd[1])
                xt = s_x2[t % 2]
                mu = s_st[:, t:t + 1]; sd = s_st[:, 24 + t:25 + t]
                rstd = s_st[:, 32 + t:33 + t]; nm = s_st[:, 40 + t:41 + t]
                v.reciprocal(rstd, sd)
                v.drain()
                v.scalar_tensor_tensor(nm, mu, -1.0, rstd, ALU.mult, ALU.mult)
                v.drain()
                v.tensor_scalar(s_xn[:], xt[:], rstd, nm, ALU.mult, ALU.add)
                v.drain()
                v.tensor_copy(s_h2[t % 2][:], s_xn[:])
                v.drain()
                v.scalar_tensor_tensor(s_l2[t % 2][:], s_xn[:], 1.0, s_h2[t % 2][:],
                                       ALU.mult, ALU.subtract)
                v.drain().then_inc(S["PV"], 1)
            emit("v", f)
            XNS = inc("PV")
            hx2.push("PV", XNS)

            def f(g, t=t, XNS=XNS):
                g.wait_ge(S["PV"], XNS)
                hs = "hl0" if t % 2 == 0 else "hl1"
                g.dma_start(xnh_in[t * 128:(t + 1) * 128, :], s_h2[t % 2][:]).then_inc(S[hs], 16)
                g.dma_start(xnl_in[t * 128:(t + 1) * 128, :], s_l2[t % 2][:]).then_inc(S[hs], 16)
            emit("g", f)
            hs = "hl0" if t % 2 == 0 else "hl1"
            STX = inc(hs, 32)
            hhl.push(hs, STX)

        XN_ST0 = cnt["hl0"]
        XN_ST1 = cnt["hl1"]

        def f(g, XN_ST0=XN_ST0, XN_ST1=XN_ST1):
            g.wait_ge(S["hl0"], XN_ST0)
            g.wait_ge(S["hl1"], XN_ST1)
            g.collective_compute("AllGather", ALU.bypass, RG,
                                 [xnh_in[0:512, :].opt()], [xnhA.ap().opt()]).then_inc(S["ccxh"])
            g.collective_compute("AllGather", ALU.bypass, RG,
                                 [xnl_in[0:512, :].opt()], [xnlA.ap().opt()]).then_inc(S["ccxl"])
            g.collective_compute("AllGather", ALU.bypass, RG,
                                 [xnh_in[512:1024, :].opt()], [xnhB.ap().opt()]).then_inc(S["ccxh2"])
            g.collective_compute("AllGather", ALU.bypass, RG,
                                 [xnl_in[512:1024, :].opt()], [xnlB.ap().opt()]).then_inc(S["ccxl2"])
        emit("g", f)
        CC_XNH = inc("ccxh")
        CC_XNL = inc("ccxl")
        CC_XNH2 = inc("ccxh2")
        CC_XNL2 = inc("ccxl2")
        XCHUNKS = [(xnhA, xnlA, i, i * 1024, "ccxh", "ccxl", CC_XNH, CC_XNL)
                   for i in range(8)]
        XCHUNKS += [(xnhB, xnlB, i, i * 1024 + 512, "ccxh2", "ccxl2", CC_XNH2, CC_XNL2)
                    for i in range(8)]

        for wi, (wsrc, hdst, ldst) in enumerate([
                (wq, wsp["wqh"], wsp["wql"]), (wk, wsp["wkh"], wsp["wkl"]),
                (wv, wsp["wvh"], wsp["wvl"]), (wo, wsp["woh_in"], wsp["wol_in"])]):
            for j in range(2):
                gd = hx2.guard()
                def f(y, wsrc=wsrc, j=j, gd=gd):
                    if gd:
                        y.wait_ge(S[gd[0]], gd[1])
                    sl = "x2a" if j == 0 else "x2b"
                    y.dma_start(s_x2[j][:], wsrc[j * 128:(j + 1) * 128, :]).then_inc(S[sl], 16)
                emit("y", f)
                sl = "x2a" if j == 0 else "x2b"
                WLD = inc(sl, 16)

                gd = hhl.guard()
                def f(v, j=j, WLD=WLD, gd=gd, sl=sl):
                    v.wait_ge(S[sl], WLD)
                    if gd:
                        v.wait_ge(S[gd[0]], gd[1])
                    v.tensor_copy(s_h2[j][:], s_x2[j][:])
                    v.drain()
                    v.scalar_tensor_tensor(s_l2[j][:], s_x2[j][:], 1.0, s_h2[j][:],
                                           ALU.mult, ALU.subtract)
                    v.drain().then_inc(S["PV"], 1)
                emit("v", f)
                WSPv = inc("PV")
                hx2.push("PV", WSPv)

                def f(g, hdst=hdst, ldst=ldst, j=j, WSPv=WSPv):
                    g.wait_ge(S["PV"], WSPv)
                    hs = "hl0" if j == 0 else "hl1"
                    g.dma_start(hdst[j * 128:(j + 1) * 128, :], s_h2[j][:]).then_inc(S[hs], 16)
                    g.dma_start(ldst[j * 128:(j + 1) * 128, :], s_l2[j][:]).then_inc(S[hs], 16)
                emit("g", f)
                hs = "hl0" if j == 0 else "hl1"
                WST = inc(hs, 32)
                hhl.push(hs, WST)

        WSPLIT_ST0 = cnt["hl0"]
        WSPLIT_ST1 = cnt["hl1"]

        def f(g, WSPLIT_ST0=WSPLIT_ST0, WSPLIT_ST1=WSPLIT_ST1):
            g.wait_ge(S["hl0"], WSPLIT_ST0)
            g.wait_ge(S["hl1"], WSPLIT_ST1)
            g.collective_compute("AllGather", ALU.bypass, RG,
                                 [wsp["woh_in"].ap().opt()], [woh.ap().opt()]).then_inc(S["ccwh"])
            g.collective_compute("AllGather", ALU.bypass, RG,
                                 [wsp["wol_in"].ap().opt()], [wol.ap().opt()]).then_inc(S["ccwl"])
        emit("g", f)
        CC_WOH = inc("ccwh")
        CC_WOL = inc("ccwl")

        WT = [(wu, (i * 128, 0)) for i in range(8)]
        WT += [(wgt, (i * 128, 0)) for i in range(8)]
        WT += [(wd, ((i // 4) * 128, (i % 4) * 2048)) for i in range(8)]
        TDST = [(tu_in, (i * 128, 0)) for i in range(8)]
        TDST += [(tg_in, (i * 128, 0)) for i in range(8)]
        TDST += [(td_in, ((i // 4) * 128, (i % 4) * 2048)) for i in range(8)]

        for k, (wsrc, (r0, c0)) in enumerate(WT):
            widx = k // 8
            gd = hq2.guard()
            qsl_ = "qw0"
            def f(y, wsrc=wsrc, r0=r0, c0=c0, k=k, gd=gd, qsl_=qsl_):
                if gd:
                    y.wait_ge(S[gd[0]], gd[1])
                y.dma_start(s_q2[k % 2][:], wsrc[r0:r0 + 128, c0:c0 + D]).then_inc(S[qsl_], 16)
            emit("y", f)
            WLD = inc(qsl_, 16)

            def f(v, k=k, widx=widx, WLD=WLD, qsl_=qsl_):
                v.wait_ge(S[qsl_], WLD)
                v.reduce_sum(s_qabs[:], s_q2[k % 2][:], axis=AX, apply_absolute_value=True)
                v.drain()
                v.tensor_add(s_qacc[:, widx:widx + 1], s_qacc[:, widx:widx + 1],
                             s_qabs[:])
                v.drain().then_inc(S["PV"], 1)
            emit("v", f)
            SUMV = inc("PV")
            hq2.push("PV", SUMV)

        SUM_DONE = cnt["PV"]

        p0.close()

        p1 = ExitStack()
        s_qh = p1.enter_context(nc.sbuf_tensor("s_qh", [128, N], BF16))
        s_ql = p1.enter_context(nc.sbuf_tensor("s_ql", [128, N], BF16))
        s_kh = p1.enter_context(nc.sbuf_tensor("s_kh", [128, N], BF16))
        s_kl = p1.enter_context(nc.sbuf_tensor("s_kl", [128, N], BF16))
        s_vtf = p1.enter_context(nc.sbuf_tensor("s_vtf", [128, N], F32))
        s_vx = p1.enter_context(nc.sbuf_tensor("s_vx", [128, 64 * 132], F32))
        s_wqt = [p1.enter_context(nc.sbuf_tensor(f"s_wqt{i}", [128, D], BF16,
                                                 is_transpose_dest=True)) for i in range(2)]
        s_wkt = [p1.enter_context(nc.sbuf_tensor(f"s_wkt{i}", [128, D], BF16,
                                                 is_transpose_dest=True)) for i in range(2)]
        s_wvt = [p1.enter_context(nc.sbuf_tensor(f"s_wvt{i}", [128, D], BF16,
                                                 is_transpose_dest=True)) for i in range(2)]
        s_xth = [p1.enter_context(nc.sbuf_tensor(f"s_xth{i}", [128, 512], BF16,
                                                 is_transpose_dest=True)) for i in range(4)]
        s_xtl = [p1.enter_context(nc.sbuf_tensor(f"s_xtl{i}", [128, 512], BF16,
                                                 is_transpose_dest=True)) for i in range(4)]
        s_ptmp = p1.enter_context(nc.sbuf_tensor("s_ptmp", [128, 512], F32))
        s_ptmp2 = p1.enter_context(nc.sbuf_tensor("s_ptmp2", [128, 512], F32))
        s_sqb = p1.enter_context(nc.sbuf_tensor("s_sqb", [128, 512], BF16))
        s_mx = p1.enter_context(nc.sbuf_tensor("s_mx", [1, 8], F32))
        s_pt = [p1.enter_context(nc.sbuf_tensor(f"s_pt{i}", [128, 512], F32)) for i in range(3)]
        s_ctxf = p1.enter_context(nc.sbuf_tensor("s_ctxf", [128, 128], F32))
        s_rz = p1.enter_context(nc.sbuf_tensor("s_rz", [128, 1], F32))
        s_cth = [p1.enter_context(nc.sbuf_tensor(f"s_cth{i}", [128, 128], BF16)) for i in range(2)]
        s_ctl = [p1.enter_context(nc.sbuf_tensor(f"s_ctl{i}", [128, 128], BF16)) for i in range(2)]

        hxt = Hist(4)
        hwt = Hist(1)
        hps = Hist(3)
        hpt = Hist(2)
        hctx = Hist(2)
        it_idx = [0]

        state = {"SCP_RDY": 0, "psA_extra_wait": 0, "QSL_LD": 0, "SCALES_V": 0,
                 "THR_BC": 0, "quant_next": 0, "QT_ST": {}, "psNX": None}

        def slot_psA(KMX_cur):
            def f(t, SUM_DONE=SUM_DONE, KMX_cur=KMX_cur):
                t.wait_ge(S["PV"], SUM_DONE)
                t.wait_ge(S["PV"], KMX_cur)
                t.matmul(state["psNX"][0:1, 0:3], s_ones_c[:], s_qacc[:, 0:3],
                         start=True, stop=True)
                t.drain().then_inc(S["PT"], 1)
            emit("t", f)
            PSA_T = inc("PT")

            def f(v, PSA_T=PSA_T):
                v.wait_ge(S["PT"], PSA_T)
                v.tensor_copy(s_qsc[:, 0:3], state["psNX"][0:1, 0:3])
                v.drain().then_inc(S["PV"], 1)
            emit("v", f)
            state["SCP_RDY"] = inc("PV")
            state["psA_extra_wait"] = state["SCP_RDY"]

            def f(g, M=state["SCP_RDY"]):
                g.wait_ge(S["PV"], M)
                g.dma_start(scp_in[:, :], s_qsc[:]).then_inc(S["gsc"], 16)
            emit("g", f)
            SCP_ST = inc("gsc", 16)

            def f(g, SCP_ST=SCP_ST):
                g.wait_ge(S["gsc"], SCP_ST)
                g.collective_compute("AllGather", ALU.bypass, RG,
                                     [scp_in.ap().opt()], [scp.ap().opt()]
                                     ).then_inc(S["ccsc"])
            emit("g", f)
            inc("ccsc")

        def slot_qsl():
            def f(g, CC=cnt["ccsc"]):
                g.wait_ge(S["ccsc"], CC)
                g.dma_start(s_qall[:], bass.AP(scp, 0, [[128, 1], [1, 128]])
                            ).then_inc(S["qsl"], 16)
            emit("g", f)
            state["QSL_LD"] = inc("qsl", 16)

        def slot_scales():
            def f(v, QSL_LD=state["QSL_LD"]):
                v.wait_ge(S["qsl"], QSL_LD)
                for w in range(3):
                    v.reduce_sum(s_qsc[:, w:w + 1],
                                 bass.AP(s_qall, w, [[128, 1], [16, 8]]), axis=AX)
                    v.drain()
                    v.tensor_scalar(s_qsc[:, 3 + w:4 + w], s_qsc[:, w:w + 1],
                                    1.0 / NEL_W, EPS_Q, ALU.mult, ALU.add)
                    v.drain()
                    v.tensor_scalar_mul(s_qsc[:, 6 + w:7 + w], s_qsc[:, 3 + w:4 + w], 0.5)
                    v.tensor_scalar_mul(s_qsc[:, 9 + w:10 + w], s_qsc[:, 3 + w:4 + w], -0.5)
                    v.drain()
                v.drain().then_inc(S["PV"], 1)
            emit("v", f)
            state["SCALES_V"] = inc("PV")

            def f(g, M=state["SCALES_V"]):
                g.wait_ge(S["PV"], M)
                g.dma_start(scw[:, :], s_qsc[:]).then_inc(S["qsw"], 16)
            emit("g", f)
            SCW_ST = inc("qsw", 16)

            def f(g, SCW_ST=SCW_ST):
                g.wait_ge(S["qsw"], SCW_ST)
                g.dma_start(s_scales[:], bass.AP(scw, 3, [[0, 128], [1, 3]])
                            ).then_inc(S["qbc"], 16)
                g.dma_start(s_thr[:], bass.AP(scw, 6, [[0, 128], [1, 6]])
                            ).then_inc(S["qbc"], 16)
            emit("g", f)
            state["THR_BC"] = inc("qbc", 32)

        def slot_quant_tile():
            k = state["quant_next"]
            if k >= len(WT):
                return
            state["quant_next"] = k + 1
            wsrc, (r0, c0) = WT[k]
            tdst, (tr0, tc0) = TDST[k]
            widx = k // 8
            qsl_ = "qw0"
            gd = hq2.guard()
            def f(y, wsrc=wsrc, r0=r0, c0=c0, k=k, gd=gd, qsl_=qsl_):
                if gd:
                    y.wait_ge(S[gd[0]], gd[1])
                y.dma_start(s_q2[k % 2][:], wsrc[r0:r0 + 128, c0:c0 + D]
                            ).then_inc(S[qsl_], 16)
            emit("y", f)
            WLD = inc(qsl_, 16)

            gq = hqt.guard()
            def f(v, k=k, widx=widx, WLD=WLD, qsl_=qsl_, gq=gq, THR=state["THR_BC"]):
                v.wait_ge(S[qsl_], WLD)
                v.wait_ge(S["qbc"], THR)
                if gq:
                    v.wait_ge(S[gq[0]], gq[1])
                v.tensor_single_scalar(s_qq[:], s_q2[k % 2][:],
                                       s_thr[:, 3 + widx:4 + widx], ALU.is_lt)
                v.drain()
                v.scalar_tensor_tensor(s_qt[k % 2][:], s_q2[k % 2][:],
                                       s_thr[:, widx:widx + 1], s_qq[:],
                                       ALU.is_gt, ALU.subtract)
                v.drain().then_inc(S["PV"], 1)
            emit("v", f)
            QV = inc("PV")
            hq2.push("PV", QV)

            tsl = "qts0"
            def f(g, tdst=tdst, tr0=tr0, tc0=tc0, k=k, QV=QV, tsl=tsl):
                g.wait_ge(S["PV"], QV)
                g.dma_start(tdst[tr0:tr0 + 128, tc0:tc0 + D],
                            s_qt[k % 2][:]).then_inc(S[tsl], 16)
            emit("g", f)
            QST = inc(tsl, 16)
            hqt.push(tsl, QST)
            state["QT_ST"][k] = (tsl, QST)

        def slot_t_ags():
            q0 = cnt["qts0"]
            q1 = cnt["qts1"]
            def f(g, q0=q0, q1=q1):
                g.wait_ge(S["qts0"], q0)
                g.wait_ge(S["qts1"], q1)
                g.collective_compute("AllGather", ALU.bypass, RG,
                                     [tu_in.ap().opt()], [tu.ap().opt()]).then_inc(S["cctu"])
                g.collective_compute("AllGather", ALU.bypass, RG,
                                     [tg_in.ap().opt()], [tgg.ap().opt()]).then_inc(S["cctg"])
                g.collective_compute("AllGather", ALU.bypass, RG,
                                     [td_in.ap().opt()], [td.ap().opt()]).then_inc(S["cctd"])
            emit("g", f)
            inc("cctu"); inc("cctg"); inc("cctd")

        for h in range(2):
            gdw = hwt.guard()
            def f(y, h=h, gdw=gdw, WSPLIT_ST0=WSPLIT_ST0, WSPLIT_ST1=WSPLIT_ST1):
                y.wait_ge(S["hl0"], WSPLIT_ST0)
                y.wait_ge(S["hl1"], WSPLIT_ST1)
                if gdw:
                    y.wait_ge(S[gdw[0]], gdw[1])
                for dc in range(16):
                    for bf, big in ((0, wsp["wqh"]), (1, wsp["wql"])):
                        y.dma_start(s_wqt[bf][:, dc * 128:(dc + 1) * 128],
                                    big[h * 128:(h + 1) * 128, dc * 128:(dc + 1) * 128],
                                    transpose=True).then_inc(S["ywt"], 16)
                    for bf, big in ((0, wsp["wkh"]), (1, wsp["wkl"])):
                        y.dma_start(s_wkt[bf][:, dc * 128:(dc + 1) * 128],
                                    big[h * 128:(h + 1) * 128, dc * 128:(dc + 1) * 128],
                                    transpose=True).then_inc(S["ywt"], 16)
                    for bf, big in ((0, wsp["wvh"]), (1, wsp["wvl"])):
                        y.dma_start(s_wvt[bf][:, dc * 128:(dc + 1) * 128],
                                    big[h * 128:(h + 1) * 128, dc * 128:(dc + 1) * 128],
                                    transpose=True).then_inc(S["ywt"], 16)
            emit("y", f)
            WT_LD = inc("ywt", 16 * 96)

            def f(v, h=h):
                v.memset(s_vx[:], 1.0)
                v.memset(s_mx[:, 0:2], 0.0)
                v.drain().then_inc(S["PV"], 1)
            emit("v", f)
            VX_INIT = inc("PV")

            pp = ExitStack()
            psQ = pp.enter_context(nc.psum_tensor(f"psQ_h{h}", [128, 512], F32))
            psK = pp.enter_context(nc.psum_tensor(f"psK_h{h}", [128, 512], F32))
            psV1 = pp.enter_context(nc.psum_tensor(f"psV1_h{h}", [128, 512], F32))
            psNX = pp.enter_context(nc.psum_tensor(f"psNX_h{h}", [128, 512], F32))
            psB = pp.enter_context(nc.psum_tensor(f"psB_h{h}", [128, 1], F32))
            psT = [pp.enter_context(nc.psum_tensor(f"psT{i}_h{h}", [128, 128], F32))
                   for i in range(2)]
            state["psNX"] = psNX

            SEVAC_prev = 0
            EVAC_prev = 0
            KSQ_prev = 0
            NRM_prev = 0
            KMX_prev = 0
            for sc in range(16):
                srcH, srcL, rb, g0, ccH, ccL, vH, vL = XCHUNKS[sc]
                dc_marks_now = [cnt["PT"] + 1 + i for i in range(16)]
                dy_dc = []
                for dc in range(16):
                    gd = hxt.guard()
                    xsl = ("xt0", "xt1", "xt2", "xt3")[dc % 4]
                    def f(y, sc=sc, dc=dc, gd=gd, WT_LD=WT_LD, xsl=xsl,
                          srcH=srcH, srcL=srcL, rb=rb, ccH=ccH, ccL=ccL,
                          vH=vH, vL=vL):
                        if sc == 0 and dc == 0:
                            y.wait_ge(S["ywt"], WT_LD)
                        if dc == 0:
                            y.wait_ge(S[ccH], vH)
                            y.wait_ge(S[ccL], vL)
                        if gd:
                            y.wait_ge(S[gd[0]], gd[1])
                        y.dma_start(s_xth[dc % 4][:],
                                    srcH[rb * 512:(rb + 1) * 512, dc * 128:(dc + 1) * 128],
                                    transpose=True).then_inc(S[xsl], 16)
                        y.dma_start(s_xtl[dc % 4][:],
                                    srcL[rb * 512:(rb + 1) * 512, dc * 128:(dc + 1) * 128],
                                    transpose=True).then_inc(S[xsl], 16)
                    emit("y", f)
                    dy_dc.append((xsl, inc(xsl, 32)))
                    hxt.push("PT", dc_marks_now[dc])

                def f(t, sc=sc, dy_dc=tuple(dy_dc), VX_INIT=VX_INIT,
                      SEVAC_prev=SEVAC_prev, EVAC_prev=EVAC_prev):
                    t.wait_ge(S["PV"], VX_INIT)
                    if SEVAC_prev:
                        t.wait_ge(S["PS"], SEVAC_prev)
                    for dc in range(16):
                        t.wait_ge(S[dy_dc[dc][0]], dy_dc[dc][1])
                        st0 = (dc == 0)
                        sp = (dc == 15)
                        wq0 = s_wqt[0][:, dc * 128:(dc + 1) * 128]
                        wq1 = s_wqt[1][:, dc * 128:(dc + 1) * 128]
                        wk0 = s_wkt[0][:, dc * 128:(dc + 1) * 128]
                        wk1 = s_wkt[1][:, dc * 128:(dc + 1) * 128]
                        wv0 = s_wvt[0][:, dc * 128:(dc + 1) * 128]
                        wv1 = s_wvt[1][:, dc * 128:(dc + 1) * 128]
                        xh = s_xth[dc % 4]; xl = s_xtl[dc % 4]
                        t.matmul(psQ[:], wq0, xh[:], start=st0, stop=False)
                        t.matmul(psQ[:], wq0, xl[:], start=False, stop=False)
                        t.matmul(psQ[:], wq1, xh[:], start=False, stop=sp)
                        t.matmul(psK[:], wk0, xh[:], start=st0, stop=False)
                        t.matmul(psK[:], wk0, xl[:], start=False, stop=False)
                        t.matmul(psK[:], wk1, xh[:], start=False, stop=sp)
                        t.matmul(psV1[:], wv0, xh[:], start=st0, stop=False)
                        t.matmul(psV1[:], wv0, xl[:], start=False, stop=False)
                        mm = t.matmul(psV1[:], wv1, xh[:], start=False, stop=sp)
                        mm.then_inc(S["PT"], 1)
                    t.drain().then_inc(S["PT"], 1)
                emit("t", f)
                dc_marks = [inc("PT") for _ in range(16)]
                assert dc_marks == dc_marks_now
                PROJ_T = inc("PT")

                def f(s, g0=g0, h=h, PROJ_T=PROJ_T, KSQ_prev=KSQ_prev, INIT_V=INIT_V):
                    s.wait_ge(S["PT"], PROJ_T)
                    s.wait_ge(S["PV"], INIT_V)
                    if KSQ_prev:
                        s.wait_ge(S["PV"], KSQ_prev)
                    s.activation(s_ptmp[:], psQ[:], AF.Identity, bias=s_bqs[:, h:h + 1],
                                 scale=ISQ)
                    s.activation(s_ptmp2[:], psK[:], AF.Identity, bias=s_bks[:, h:h + 1])
                    s.activation(s_vtf[:, g0:g0 + 512], psV1[:], AF.Identity,
                                 bias=s_bvs[:, h:h + 1])
                    s.drain().then_inc(S["PS"], 1)
                emit("s", f)
                SEVAC = inc("PS")

                def f(v, g0=g0, h=h, SEVAC=SEVAC, NRM_prev=NRM_prev):
                    v.wait_ge(S["PS"], SEVAC)
                    if NRM_prev:
                        v.wait_ge(S["PT"], NRM_prev)
                    c0 = g0
                    v.tensor_copy(s_qh[:, c0:c0 + 512], s_ptmp[:])
                    v.tensor_copy(s_kh[:, c0:c0 + 512], s_ptmp2[:])
                    v.drain()
                    v.scalar_tensor_tensor(s_ql[:, c0:c0 + 512], s_ptmp[:], 1.0,
                                           s_qh[:, c0:c0 + 512], ALU.mult, ALU.subtract)
                    v.scalar_tensor_tensor(s_kl[:, c0:c0 + 512], s_ptmp2[:], 1.0,
                                           s_kh[:, c0:c0 + 512], ALU.mult, ALU.subtract)
                    v.tensor_mul(s_sqb[:], s_ptmp[:], s_ptmp[:])
                    v.drain().then_inc(S["PV"], 1)
                emit("v", f)
                EVAC = inc("PV")

                def f(t, EVAC=EVAC, KMX_prev=KMX_prev,
                      XNQ=state["psA_extra_wait"] if sc == 3 and h == 0 else 0):
                    t.wait_ge(S["PV"], EVAC)
                    if KMX_prev:
                        t.wait_ge(S["PV"], KMX_prev)
                    if XNQ:
                        t.wait_ge(S["PV"], XNQ)
                    t.matmul(psNX[0:1, :], s_ones_cb[:], s_sqb[:], start=True, stop=True)
                    t.drain().then_inc(S["PT"], 1)
                emit("t", f)
                NQ_T = inc("PT")

                def f(v, NQ_T=NQ_T):
                    v.wait_ge(S["PT"], NQ_T)
                    v.reduce_max(s_mx[:, 2:3], psNX[0:1, :], axis=AX)
                    v.drain()
                    v.tensor_max(s_mx[:, 0:1], s_mx[:, 0:1], s_mx[:, 2:3])
                    v.tensor_mul(s_sqb[:], s_ptmp2[:], s_ptmp2[:])
                    v.drain().then_inc(S["PV"], 1)
                emit("v", f)
                KSQ = inc("PV")

                def f(t, KSQ=KSQ):
                    t.wait_ge(S["PV"], KSQ)
                    t.matmul(psNX[0:1, :], s_ones_cb[:], s_sqb[:], start=True, stop=True)
                    t.drain().then_inc(S["PT"], 1)
                emit("t", f)
                NRM = inc("PT")

                def f(v, NRM=NRM):
                    v.wait_ge(S["PT"], NRM)
                    v.reduce_max(s_mx[:, 2:3], psNX[0:1, :], axis=AX)
                    v.drain()
                    v.tensor_max(s_mx[:, 1:2], s_mx[:, 1:2], s_mx[:, 2:3])
                    v.drain().then_inc(S["PV"], 1)
                emit("v", f)
                KMX = inc("PV")

                SEVAC_prev, EVAC_prev, KSQ_prev, NRM_prev, KMX_prev = \
                    SEVAC, EVAC, KSQ, NRM, KMX

                if h == 0:
                    if sc == 2:
                        slot_psA(KMX)
                    elif sc == 3:
                        slot_qsl()
                    elif sc == 4:
                        slot_scales()

            htp = Hist(2)
            for kt in range(64):
                gtp = htp.guard()
                def f(t, kt=kt, gtp=gtp, SEVAC_prev=SEVAC_prev, VX=VX_INIT,
                      MD=MASKS_DONE):
                    if kt == 0:
                        t.wait_ge(S["PS"], SEVAC_prev)
                        t.wait_ge(S["PG"], MD)
                    if gtp:
                        t.wait_ge(S[gtp[0]], gtp[1])
                    t.transpose(psT[kt % 2][:], s_vtf[:, kt * 128:(kt + 1) * 128],
                                s_ident[:])
                    t.drain().then_inc(S["PT"], 1)
                emit("t", f)
                TPD = inc("PT")

                def f(v, kt=kt, TPD=TPD, VX=VX_INIT):
                    v.wait_ge(S["PT"], TPD)
                    v.wait_ge(S["PV"], VX)
                    v.tensor_copy(s_vx[:, kt * 132:kt * 132 + 128], psT[kt % 2][:])
                    v.drain().then_inc(S["PV"], 1)
                emit("v", f)
                TPV = inc("PV")
                htp.push("PV", TPV)

            def f(v, KMX_prev=KMX_prev):
                v.wait_ge(S["PV"], KMX_prev)
                v.tensor_mul(s_mx[:, 3:4], s_mx[:, 0:1], s_mx[:, 1:2])
                v.drain().then_inc(S["PV"], 1)
            emit("v", f)
            MH1 = inc("PV")

            def f(s, MH1=MH1):
                s.wait_ge(S["PV"], MH1)
                s.activation(s_mx[:, 4:5], s_mx[:, 3:4], AF.Sqrt)
                s.drain().then_inc(S["PS"], 1)
            emit("s", f)
            MH2 = inc("PS")

            def f(v, MH2=MH2):
                v.wait_ge(S["PS"], MH2)
                v.tensor_scalar_mul(s_mx[:, 5:6], s_mx[:, 4:5], -1.01)
                v.drain().then_inc(S["PV"], 1)
            emit("v", f)
            MH3 = inc("PV")

            def f(t, MH3=MH3):
                t.wait_ge(S["PV"], MH3)
                t.matmul(psB[:], s_ones_r[:], s_mx[:, 5:6], start=True, stop=True)
                t.drain().then_inc(S["PT"], 1)
            emit("t", f)
            MH4 = inc("PT")

            def f(v, MH4=MH4):
                v.wait_ge(S["PT"], MH4)
                v.tensor_copy(s_negmh[:], psB[:])
                v.drain().then_inc(S["PV"], 1)
            emit("v", f)
            MH_RDY = inc("PV")

            hwt.push("PT", PROJ_T)
            pp.close()

            pa = ExitStack()
            psS = [pa.enter_context(nc.psum_tensor(f"psS{i}_h{h}", [128, 512], F32))
                   for i in range(3)]
            psC = [pa.enter_context(nc.psum_tensor(f"psC{i}_h{h}", [128, 129], F32))
                   for i in range(4)]

            CTXV_prev_qc = [0, 0, 0, 0]
            for qc in range(16):
                nkt = 4 * qc + 4
                pending_pv = None
                for kt in range(nkt):
                    it = it_idx[0]
                    it_idx[0] += 1
                    bS = it % 3
                    gps = hps.guard()
                    def f(t, qc=qc, kt=kt, bS=bS, gps=gps, MH_RDY=MH_RDY,
                          CV=tuple(CTXV_prev_qc)):
                        if kt == 0:
                            t.wait_ge(S["PV"], MH_RDY)
                            for cv in CV:
                                if cv:
                                    t.wait_ge(S["PV"], cv)
                        if gps:
                            t.wait_ge(S[gps[0]], gps[1])
                        q0 = qc * 512
                        kc = slice(kt * 128, (kt + 1) * 128)
                        t.matmul(psS[bS][:], s_kh[:, kc], s_qh[:, q0:q0 + 512],
                                 start=True, stop=False)
                        t.matmul(psS[bS][:], s_kh[:, kc], s_ql[:, q0:q0 + 512],
                                 start=False, stop=False)
                        t.matmul(psS[bS][:], s_kl[:, kc], s_qh[:, q0:q0 + 512],
                                 start=False, stop=True).then_inc(S["PT"], 1)
                    emit("t", f)
                    SCO = inc("PT")

                    gpt = hpt.guard()
                    def f(s, bS=bS, SCO=SCO, gpt=gpt):
                        s.wait_ge(S["PT"], SCO)
                        if gpt:
                            s.wait_ge(S[gpt[0]], gpt[1])
                        s.activation(s_pt[bS][:], psS[bS][:], AF.Exp,
                                     bias=s_negmh[:, 0:1]).then_inc(S["PS"], 1)
                    emit("s", f)
                    EXPM = inc("PS")
                    hps.push("PS", EXPM)

                    MSK = None
                    if kt >= 4 * qc:
                        dlt = kt - 4 * qc
                        def f(v, bS=bS, dlt=dlt, EXPM=EXPM, MASKS_DONE=MASKS_DONE):
                            v.wait_ge(S["PS"], EXPM)
                            v.wait_ge(S["PG"], MASKS_DONE)
                            v.tensor_mul(s_pt[bS][:], s_pt[bS][:],
                                         s_mask[:, dlt * 512:(dlt + 1) * 512]
                                         ).then_inc(S["PV"], 1)
                        emit("v", f)
                        MSK = inc("PV")

                    def fpv(t, qc=qc, kt=kt, bS=bS, nkt=nkt, EXPM=EXPM, MSK=MSK):
                        t.wait_ge(S["PS"], EXPM)
                        if MSK is not None:
                            t.wait_ge(S["PV"], MSK)
                        for sub in range(4):
                            mm = t.matmul(psC[sub][:],
                                          s_pt[bS][:, sub * 128:(sub + 1) * 128],
                                          s_vx[:, kt * 132:kt * 132 + 129],
                                          start=(kt == 0), stop=(kt == nkt - 1))
                        mm.then_inc(S["PT"], 1)
                    if pending_pv is not None:
                        emit("t", pending_pv)
                        PVDp = inc("PT")
                        hpt.push("PT", PVDp)
                    pending_pv = fpv
                emit("t", pending_pv)
                PVDl = inc("PT")
                hpt.push("PT", PVDl)

                LASTPV = cnt["PT"]
                for sub in range(4):
                    rot = sub % 2
                    gc = hctx.guard()
                    def f(v, qc=qc, sub=sub, rot=rot, LASTPV=LASTPV, gc=gc):
                        v.wait_ge(S["PT"], LASTPV)
                        if gc:
                            v.wait_ge(S[gc[0]], gc[1])
                        v.reciprocal(s_rz[:], psC[sub][:, 128:129])
                        v.drain()
                        v.tensor_single_scalar(s_ctxf[:], psC[sub][:, 0:128],
                                               s_rz[:, 0:1], ALU.mult)
                        v.drain()
                        v.tensor_copy(s_cth[rot][:], s_ctxf[:])
                        v.drain()
                        v.scalar_tensor_tensor(s_ctl[rot][:], s_ctxf[:], 1.0,
                                               s_cth[rot][:], ALU.mult, ALU.subtract)
                        v.drain().then_inc(S["PV"], 1)
                    emit("v", f)
                    CTXV = inc("PV")
                    CTXV_prev_qc[sub] = CTXV

                    csl = "ctx0" if rot == 0 else "ctx1"
                    def f(g, qc=qc, sub=sub, rot=rot, h=h, CTXV=CTXV, csl=csl):
                        g.wait_ge(S["PV"], CTXV)
                        r0 = qc * 512 + sub * 128
                        g.dma_start(ctxh_in[r0:r0 + 128, h * 128:(h + 1) * 128],
                                    s_cth[rot][:]).then_inc(S[csl], 16)
                        g.dma_start(ctxl_in[r0:r0 + 128, h * 128:(h + 1) * 128],
                                    s_ctl[rot][:]).then_inc(S[csl], 16)
                    emit("g", f)
                    CTXST = inc(csl, 32)
                    hctx.push(csl, CTXST)

                if h == 0:
                    if qc <= 11:
                        slot_quant_tile()
                        slot_quant_tile()
                    elif qc == 12:
                        assert state["quant_next"] == len(WT)
                        slot_t_ags()
            pa.close()

        CTX_ST_ALL0 = cnt["ctx0"]
        CTX_ST_ALL1 = cnt["ctx1"]

        p1.close()
        pq.close()
        barrier()

        CTXAG_DELAY = {}

        def f(g, CTX_ST_ALL0=CTX_ST_ALL0, CTX_ST_ALL1=CTX_ST_ALL1):
            g.wait_ge(S["ctx0"], CTX_ST_ALL0)
            g.wait_ge(S["ctx1"], CTX_ST_ALL1)
            if "pt" in CTXAG_DELAY:
                g.wait_ge(S["PT"], CTXAG_DELAY["pt"])
            g.collective_compute("AllGather", ALU.bypass, RG,
                                 [ctxh_in.ap().opt()], [ctxh.ap().opt()]).then_inc(S["ccch"])
            g.collective_compute("AllGather", ALU.bypass, RG,
                                 [ctxl_in.ap().opt()], [ctxl.ap().opt()]).then_inc(S["cccl"])
        emit("g", f)
        CC_CTH = inc("ccch")
        CC_CTL = inc("cccl")

        def f(g, CC_CTH=CC_CTH, CC_CTL=CC_CTL):
            g.wait_ge(S["ccch"], CC_CTH)
            g.wait_ge(S["cccl"], CC_CTL)
            for r in range(8):
                g.reg_mov(tmpr, r * N * 256)
                g.reg_add(tmpr, tmpr, pidoff)
                g.dma_start(myctxh[0:TOK, r * 256:(r + 1) * 256],
                            bass.AP(ctxh, tmpr, [[256, TOK], [1, 256]])).then_inc(S["grp"], 16)
                g.reg_mov(tmpr, r * N * 256)
                g.reg_add(tmpr, tmpr, pidoff)
                g.dma_start(myctxl[0:TOK, r * 256:(r + 1) * 256],
                            bass.AP(ctxl, tmpr, [[256, TOK], [1, 256]])).then_inc(S["grp"], 16)
        emit("g", f)
        REPACK = inc("grp", 16 * 16)

        p2 = ExitStack()
        s_fxnh = p2.enter_context(nc.sbuf_tensor("s_fxnh", [128, 16 * 512], BF16,
                                                 is_transpose_dest=True))
        s_fxnl = p2.enter_context(nc.sbuf_tensor("s_fxnl", [128, 16 * 512], BF16,
                                                 is_transpose_dest=True))
        s_hidh = p2.enter_context(nc.sbuf_tensor("s_hidh", [128, 32 * 512], BF16))
        s_hidl = p2.enter_context(nc.sbuf_tensor("s_hidl", [128, 32 * 512], BF16))
        s_yacc = p2.enter_context(nc.sbuf_tensor("s_yacc", [128, 4 * D], F32))
        s_tu2 = [p2.enter_context(nc.sbuf_tensor(f"s_tu2_{i}", [128, 16 * 256], BF16,
                                                 is_transpose_dest=True)) for i in range(3)]
        s_tg2 = [p2.enter_context(nc.sbuf_tensor(f"s_tg2_{i}", [128, 16 * 256], BF16,
                                                 is_transpose_dest=True)) for i in range(3)]
        s_td2 = [p2.enter_context(nc.sbuf_tensor(f"s_td2_{i}", [128, 512], BF16,
                                                 is_transpose_dest=True)) for i in range(6)]
        s_silu = p2.enter_context(nc.sbuf_tensor("s_silu", [128, 512], F32))
        s_hidf = p2.enter_context(nc.sbuf_tensor("s_hidf", [128, 512], F32))
        s_outt = [p2.enter_context(nc.sbuf_tensor(f"s_outt{i}", [128, 512], F32))
                  for i in range(2)]
        psU = [p2.enter_context(nc.psum_tensor(f"psU{i}", [128, 512], F32)) for i in range(2)]
        psG2 = [p2.enter_context(nc.psum_tensor(f"psG2_{i}", [128, 512], F32)) for i in range(2)]
        psD = [p2.enter_context(nc.psum_tensor(f"psD{i}", [128, 512], F32)) for i in range(4)]

        htu = Hist(3)
        htd = Hist(6)
        tu_i = [0]
        td_i = [0]
        hyf = Hist(2)
        UGT_last_sc = 0
        HIDV_hist = Hist(2)
        CC_TU = cnt["cctu"]; CC_TG = cnt["cctg"]; CC_TD = cnt["cctd"]
        TLD_all = {}
        FXN_INFO = {}

        def emit_tld(sc_, es_):
            ti = tu_i[0]
            tb = ti % 3
            tu_i[0] += 1
            gtu = htu.marks[ti - 3] if ti >= 3 else None
            def f(e_, es=es_, gtu=gtu, tb=tb):
                if gtu:
                    e_.wait_ge(S[gtu[0]], gtu[1])
                for dc in range(16):
                    sl = S[f"tu{tb}{dc // 4}"]
                    e_.dma_start(s_tu2[tb][:, dc * 256:(dc + 1) * 256],
                                 tu[es * 256:(es + 1) * 256,
                                    dc * 128:(dc + 1) * 128],
                                 transpose=True).then_inc(sl, 16)
                    e_.dma_start(s_tg2[tb][:, dc * 256:(dc + 1) * 256],
                                 tgg[es * 256:(es + 1) * 256,
                                     dc * 128:(dc + 1) * 128],
                                 transpose=True).then_inc(sl, 16)
            emit("y", f)
            info = []
            for g in range(4):
                sl = f"tu{tb}{g}"
                info.append((sl, inc(sl, 16 * 8)))
            TLD_all[(sc_, es_)] = (tb, info)

        FX_SLOTS = ("yfx", "yfxb", "yfxc", "yfxd")

        def emit_fxn(sc_, guard_):
            info = []
            for g in range(4):
                def f(y, sc=sc_, g=g, guard_=guard_):
                    if g == 0:
                        if guard_ is None:
                            y.wait_ge(S["cctu"], CC_TU)
                            y.wait_ge(S["cctg"], CC_TG)
                            y.wait_ge(S["cctd"], CC_TD)
                        else:
                            y.wait_ge(S["PT"], guard_)
                    for dc in range(g * 4, g * 4 + 4):
                        y.dma_start(s_fxnh[:, dc * 512:(dc + 1) * 512],
                                    xnh_in[sc * 512:(sc + 1) * 512,
                                           dc * 128:(dc + 1) * 128],
                                    transpose=True).then_inc(S[FX_SLOTS[g]], 16)
                        y.dma_start(s_fxnl[:, dc * 512:(dc + 1) * 512],
                                    xnl_in[sc * 512:(sc + 1) * 512,
                                           dc * 128:(dc + 1) * 128],
                                    transpose=True).then_inc(S[FX_SLOTS[g]], 16)
                emit("y", f)
                info.append((FX_SLOTS[g], inc(FX_SLOTS[g], 16 * 8)))
            FXN_INFO[sc_] = info

        for sc in range(2):
            if sc == 0:
                emit_fxn(0, None)
            elif 1 not in FXN_INFO:
                emit_fxn(1, UGT_last_sc)

            DOWN_prev = 0
            for eh in range(2):
                for et in range(32):
                    eg = eh * 32 + et
                    es = eg // 2
                    if eg % 2 == 0 and (sc, es) not in TLD_all:
                        emit_tld(sc, es)
                    TLD = TLD_all[(sc, es)]

                    ghid = HIDV_hist.guard()
                    def f(t, sc=sc, eg=eg, es=es, et=et, FXN=FXN_INFO[sc],
                          TLD=TLD, ghid=ghid):
                        if ghid:
                            t.wait_ge(S[ghid[0]], ghid[1])
                        el = (eg % 2) * 128
                        bU = eg % 2
                        tb = TLD[0]
                        for dc in range(16):
                            if dc % 4 == 0:
                                g = dc // 4
                                if eg == 0:
                                    t.wait_ge(S[FXN[g][0]], FXN[g][1])
                                if eg % 2 == 0:
                                    t.wait_ge(S[TLD[1][g][0]], TLD[1][g][1])
                            lu = s_tu2[tb][:, dc * 256 + el: dc * 256 + el + 128]
                            lg = s_tg2[tb][:, dc * 256 + el: dc * 256 + el + 128]
                            rh = s_fxnh[:, dc * 512:(dc + 1) * 512]
                            rl = s_fxnl[:, dc * 512:(dc + 1) * 512]
                            t.matmul(psU[bU][:], lu, rh, start=(dc == 0), stop=False)
                            t.matmul(psU[bU][:], lu, rl, start=False, stop=(dc == 15))
                            t.matmul(psG2[bU][:], lg, rh, start=(dc == 0), stop=False)
                            mm = t.matmul(psG2[bU][:], lg, rl, start=False, stop=(dc == 15))
                        mm.then_inc(S["PT"], 1)
                    emit("t", f)
                    UGT = inc("PT")
                    if sc == 0 and eg == 31:
                        CTXAG_DELAY["pt"] = UGT
                    if eg == 63:
                        UGT_last_sc = UGT
                    if eg % 2 == 1:
                        htu.push("PT", UGT)

                    def f(s, eg=eg, UGT=UGT, THR=state["THR_BC"]):
                        s.wait_ge(S["PT"], UGT)
                        s.wait_ge(S["qbc"], THR)
                        s.activation(s_silu[:], psU[eg % 2][:], AF.Silu,
                                     scale=s_scales[:, 0:1]).then_inc(S["PS"], 1)
                    emit("s", f)
                    SIL = inc("PS")

                    def f(v, et=et, eg=eg, SIL=SIL):
                        v.wait_ge(S["PS"], SIL)
                        v.scalar_tensor_tensor(s_hidf[:], psG2[eg % 2][:],
                                               s_scales[:, 1:2], s_silu[:],
                                               ALU.mult, ALU.mult)
                        c0 = et * 512
                        v.drain()
                        v.tensor_copy(s_hidh[:, c0:c0 + 512], s_hidf[:])
                        v.drain()
                        v.scalar_tensor_tensor(s_hidl[:, c0:c0 + 512], s_hidf[:], 1.0,
                                               s_hidh[:, c0:c0 + 512],
                                               ALU.mult, ALU.subtract)
                        v.drain().then_inc(S["PV"], 1)
                    emit("v", f)
                    HIDV = inc("PV")
                    HIDV_hist.push("PV", HIDV)

                HID_HALF = cnt["PV"]

                for dc in range(4):
                    for ec in range(32):
                        if dc == 0 and ec == 6:
                            if eh == 0 and HOIST_EH in (1, 3):
                                for e2 in range(16, 19):
                                    if (sc, e2) not in TLD_all:
                                        emit_tld(sc, e2)
                            elif eh == 1 and sc == 0 and HOIST_EH in (2, 3):
                                emit_fxn(1, UGT_last_sc)
                                for e2 in range(3):
                                    if (1, e2) not in TLD_all:
                                        emit_tld(1, e2)
                        ege = eh * 32 + ec
                        db = td_i[0] % 6
                        td_i[0] += 1
                        gtd = htd.guard()
                        dsl = ("td0", "td1", "td2", "td3", "td4", "td5")[db]
                        def f(y, sc=sc, dc=dc, ec=ec, ege=ege, gtd=gtd, dsl=dsl, db=db):
                            if gtd:
                                y.wait_ge(S[gtd[0]], gtd[1])
                            for half in range(2):
                                y.dma_start(
                                    s_td2[db][:, half * 256:(half + 1) * 256],
                                    td[dc * 512 + half * 256:dc * 512 + (half + 1) * 256,
                                       ege * 128:(ege + 1) * 128],
                                    transpose=True).then_inc(S[dsl], 16)
                        emit("y", f)
                        DT_LD = (dsl, inc(dsl, 32), db)

                        def f(t, dc=dc, ec=ec, DT_LD=DT_LD, HID_HALF=HID_HALF,
                              DOWN_prev=DOWN_prev):
                            t.wait_ge(S[DT_LD[0]], DT_LD[1])
                            if ec == 0:
                                t.wait_ge(S["PV"], HID_HALF)
                                if DOWN_prev:
                                    t.wait_ge(S["PV"], DOWN_prev)
                            db = DT_LD[2]
                            for st in range(4):
                                lh = s_hidh[:, ec * 512 + st * 128:ec * 512 + (st + 1) * 128]
                                ll = s_hidl[:, ec * 512 + st * 128:ec * 512 + (st + 1) * 128]
                                t.matmul(psD[st][:], lh, s_td2[db][:],
                                         start=(ec == 0), stop=False)
                                mm = t.matmul(psD[st][:], ll, s_td2[db][:],
                                              start=False, stop=(ec == 31))
                            mm.then_inc(S["PT"], 1)
                        emit("t", f)
                        DMM = inc("PT")
                        htd.push("PT", DMM)

                    DMM_dc = cnt["PT"]
                    for st in range(4):
                        if eh == 0:
                            def f(v, sc=sc, dc=dc, st=st, DMM_dc=DMM_dc):
                                v.wait_ge(S["PT"], DMM_dc)
                                v.tensor_copy(
                                    s_yacc[:, st * D + dc * 512:st * D + (dc + 1) * 512],
                                    psD[st][:])
                                v.drain().then_inc(S["PV"], 1)
                            emit("v", f)
                            inc("PV")
                        else:
                            gyf = hyf.guard()
                            def f(v, sc=sc, dc=dc, st=st, DMM_dc=DMM_dc, gyf=gyf):
                                v.wait_ge(S["PT"], DMM_dc)
                                if gyf:
                                    v.wait_ge(S[gyf[0]], gyf[1])
                                v.tensor_add(s_outt[st % 2][:], psD[st][:],
                                             s_yacc[:, st * D + dc * 512:
                                                    st * D + (dc + 1) * 512])
                                v.drain().then_inc(S["PV"], 1)
                            emit("v", f)
                            YEVv = inc("PV")

                            ysl = "yf0" if st % 2 == 0 else "yf1"
                            def f(g, sc=sc, dc=dc, st=st, YEVv=YEVv, ysl=ysl):
                                g.wait_ge(S["PV"], YEVv)
                                r0 = sc * 512 + st * 128
                                g.dma_start(yff[r0:r0 + 128, dc * 512:(dc + 1) * 512],
                                            s_outt[st % 2][:]).then_inc(S[ysl], 16)
                            emit("g", f)
                            YST = inc(ysl, 16)
                            hyf.push(ysl, YST)

                    DOWN_prev = cnt["PV"]

        YFF_ST0 = cnt["yf0"]
        YFF_ST1 = cnt["yf1"]
        p2.close()
        barrier()

        po = ExitStack()
        s_ctTh = po.enter_context(nc.sbuf_tensor("s_ctTh", [128, 16 * TOK], BF16,
                                                 is_transpose_dest=True))
        s_ctTl = po.enter_context(nc.sbuf_tensor("s_ctTl", [128, 16 * TOK], BF16,
                                                 is_transpose_dest=True))
        s_woTh = [po.enter_context(nc.sbuf_tensor(f"s_woTh{i}", [128, 16 * 512], BF16,
                                                  is_transpose_dest=True)) for i in range(2)]
        s_woTl = [po.enter_context(nc.sbuf_tensor(f"s_woTl{i}", [128, 16 * 512], BF16,
                                                  is_transpose_dest=True)) for i in range(2)]
        s_xr = [po.enter_context(nc.sbuf_tensor(f"s_xr{i}", [128, 512], F32))
                for i in range(2)]
        s_yf = [po.enter_context(nc.sbuf_tensor(f"s_yf{i}", [128, 512], F32))
                for i in range(2)]
        s_b1 = po.enter_context(nc.sbuf_tensor("s_b1", [128, 512], F32))
        s_oE = [po.enter_context(nc.sbuf_tensor(f"s_oE{i}", [128, 512], F32))
                for i in range(2)]
        psO = [po.enter_context(nc.psum_tensor(f"psO{i}", [128, 512], F32))
               for i in range(2)]

        CT_SLOTS = ("yct", "yctb", "yctc", "yctd")

        def f(y, REPACK=REPACK):
            y.wait_ge(S["grp"], REPACK)
            for ic in range(16):
                sl = S[CT_SLOTS[ic // 4]]
                y.dma_start(s_ctTh[:, ic * TOK:(ic + 1) * TOK],
                            myctxh[0:TOK, ic * 128:(ic + 1) * 128],
                            transpose=True).then_inc(sl, 16)
                y.dma_start(s_ctTl[:, ic * TOK:(ic + 1) * TOK],
                            myctxl[0:TOK, ic * 128:(ic + 1) * 128],
                            transpose=True).then_inc(sl, 16)
        emit("y", f)
        CT_INFO = [(CT_SLOTS[g], inc(CT_SLOTS[g], 16 * 8)) for g in range(4)]

        ho_ev = Hist(2)
        hxr = Hist(2)
        hot = Hist(2)
        OPROJ_prev = [0, 0]
        for oc in range(4):
            gdw = OPROJ_prev[oc % 2]
            def f(y, oc=oc, gdw=gdw, CC_WOH=CC_WOH, CC_WOL=CC_WOL):
                y.wait_ge(S["ccwh"], CC_WOH)
                y.wait_ge(S["ccwl"], CC_WOL)
                if gdw:
                    y.wait_ge(S["PT"], gdw)
                for ic in range(16):
                    sl = S[f"wo{oc % 2}{ic // 4}"]
                    y.dma_start(s_woTh[oc % 2][:, ic * 512:ic * 512 + 512],
                                woh[oc * 512:(oc + 1) * 512, ic * 128:(ic + 1) * 128],
                                transpose=True).then_inc(sl, 16)
                    y.dma_start(s_woTl[oc % 2][:, ic * 512:ic * 512 + 512],
                                wol[oc * 512:(oc + 1) * 512, ic * 128:(ic + 1) * 128],
                                transpose=True).then_inc(sl, 16)
            emit("y", f)
            WO_INFO = [(f"wo{oc % 2}{g}", inc(f"wo{oc % 2}{g}", 16 * 8))
                       for g in range(4)]

            for st in range(8):
                bO = st % 2
                gxr = hxr.guard()
                rsl = "xr0" if st % 2 == 0 else "xr1"
                def f(y, oc=oc, st=st, gxr=gxr, rsl=rsl, Y0=YFF_ST0, Y1=YFF_ST1):
                    y.wait_ge(S["yf0"], Y0)
                    y.wait_ge(S["yf1"], Y1)
                    if gxr:
                        y.wait_ge(S[gxr[0]], gxr[1])
                    y.dma_start(s_xr[st % 2][:],
                                xs[st * 128:(st + 1) * 128, oc * 512:(oc + 1) * 512]
                                ).then_inc(S[rsl], 16)
                    y.dma_start(s_yf[st % 2][:],
                                yff[st * 128:(st + 1) * 128, oc * 512:(oc + 1) * 512]
                                ).then_inc(S[rsl], 16)
                emit("y", f)
                RES_LD = (rsl, inc(rsl, 32))

                gev = ho_ev.guard()
                def f(t, oc=oc, st=st, bO=bO, WO=WO_INFO, CT=CT_INFO, gev=gev):
                    if gev:
                        t.wait_ge(S[gev[0]], gev[1])
                    for ic in range(16):
                        if ic % 4 == 0 and st == 0:
                            g = ic // 4
                            if oc == 0:
                                t.wait_ge(S[CT[g][0]], CT[g][1])
                            t.wait_ge(S[WO[g][0]], WO[g][1])
                        cth = s_ctTh[:, ic * TOK + st * 128: ic * TOK + (st + 1) * 128]
                        ctl = s_ctTl[:, ic * TOK + st * 128: ic * TOK + (st + 1) * 128]
                        wh = s_woTh[oc % 2][:, ic * 512:ic * 512 + 512]
                        wl = s_woTl[oc % 2][:, ic * 512:ic * 512 + 512]
                        t.matmul(psO[bO][:], cth, wh, start=(ic == 0), stop=False)
                        t.matmul(psO[bO][:], cth, wl, start=False, stop=False)
                        mm = t.matmul(psO[bO][:], ctl, wh, start=False, stop=(ic == 15))
                    mm.then_inc(S["PT"], 1)
                emit("t", f)
                OMM = inc("PT")
                if st == 7:
                    OPROJ_prev[oc % 2] = OMM

                got = hot.guard()
                def f(v, bO=bO, oc=oc, st=st, OMM=OMM, got=got, RES_LD=RES_LD):
                    v.wait_ge(S["PT"], OMM)
                    v.wait_ge(S[RES_LD[0]], RES_LD[1])
                    if got:
                        v.wait_ge(S[got[0]], got[1])
                    v.tensor_add(s_b1[:], psO[bO][:], s_bob[:, oc * 512:(oc + 1) * 512])
                    v.drain()
                    v.tensor_add(s_b1[:], s_b1[:], s_xr[st % 2][:])
                    v.drain()
                    v.scalar_tensor_tensor(s_oE[bO][:], s_yf[st % 2][:],
                                           s_scales[:, 2:3], s_b1[:],
                                           ALU.mult, ALU.add)
                    v.drain().then_inc(S["PV"], 1)
                emit("v", f)
                OEV = inc("PV")
                ho_ev.push("PV", OEV)
                hxr.push("PV", OEV)

                osl = "ou0" if bO == 0 else "ou1"
                def f(g, oc=oc, st=st, bO=bO, OEV=OEV, osl=osl):
                    g.wait_ge(S["PV"], OEV)
                    g.dma_start(out[st * 128:(st + 1) * 128, oc * 512:(oc + 1) * 512],
                                s_oE[bO][:]).then_inc(S[osl], 16)
                emit("g", f)
                OST = inc(osl, 16)
                hot.push(osl, OST)

        if DEBUG:
            F0 = cnt["ou0"]; F1 = cnt["ou1"]
            def f(g, F0=F0, F1=F1):
                g.wait_ge(S["ou0"], F0)
                g.wait_ge(S["ou1"], F1)
                g.dma_start(dbg["dbg_xnh"][:, :], xnh_in[:, :]).then_inc(S["gsc"], 16)
                g.dma_start(dbg["dbg_ctxh"][:, :], ctxh_in[:, :]).then_inc(S["gsc"], 16)
                g.dma_start(dbg["dbg_yff"][:, :], yff[:, :]).then_inc(S["gsc"], 16)
                g.dma_start(dbg["dbg_scp"][:, :], scp[:, :]).then_inc(S["gsc"], 16)
                g.dma_start(dbg["dbg_tu"][:, :], tu_in[:, :]).then_inc(S["gsc"], 16)
            emit("g", f)
            DBG_ST = inc("gsc", 16 * 5)
            def f(y, DBG_ST=DBG_ST):
                y.wait_ge(S["gsc"], DBG_ST)
            emit("y", f)

        FIN0 = cnt["ou0"]
        FIN1 = cnt["ou1"]

        def f(y, FIN0=FIN0, FIN1=FIN1):
            y.wait_ge(S["ou0"], FIN0)
            y.wait_ge(S["ou1"], FIN1)
        emit("y", f)

        po.close()

        with nc.Block() as block:
            @block.sync
            def _(y):
                for fn in ops["y"]:
                    fn(y)

            @block.tensor
            def _(t):
                for fn in ops["t"]:
                    fn(t)

            @block.vector
            def _(v):
                for fn in ops["v"]:
                    fn(v)

            @block.scalar
            def _(s):
                for fn in ops["s"]:
                    fn(s)

            @block.gpsimd
            def _(g):
                for fn in ops["g"]:
                    fn(g)

    n_dedup = dedupe_ldweights(nc)
    return nc


_NC_CACHE = None


def kernel(L0_attn_x, L0_attn_wq, L0_attn_bq, L0_attn_wk, L0_attn_bk,
           L0_attn_wv, L0_attn_bv, L0_attn_wo, L0_attn_bo,
           L0_ffn_W_up, L0_ffn_W_gate, L0_ffn_W_down):
    global _NC_CACHE
    if _NC_CACHE is None:
        _NC_CACHE = build_program()
    nc = _NC_CACHE

    x = np.ascontiguousarray(np.asarray(L0_attn_x)).reshape(N, D)
    wq_ = np.asarray(L0_attn_wq)
    wk_ = np.asarray(L0_attn_wk)
    wv_ = np.asarray(L0_attn_wv)
    wo_ = np.asarray(L0_attn_wo)
    bq_ = np.asarray(L0_attn_bq).reshape(D, 1)
    bk_ = np.asarray(L0_attn_bk).reshape(D, 1)
    bv_ = np.asarray(L0_attn_bv).reshape(D, 1)
    bo_ = np.ascontiguousarray(np.asarray(L0_attn_bo)).reshape(1, D)
    wu_ = np.asarray(L0_ffn_W_up)
    wg_ = np.asarray(L0_ffn_W_gate)
    wd_ = np.asarray(L0_ffn_W_down)

    in_maps = []
    for c in range(8):
        in_maps.append(dict(
            xs=np.ascontiguousarray(x[c * TOK:(c + 1) * TOK]),
            wq=np.ascontiguousarray(wq_[c * 256:(c + 1) * 256]),
            wk=np.ascontiguousarray(wk_[c * 256:(c + 1) * 256]),
            wv=np.ascontiguousarray(wv_[c * 256:(c + 1) * 256]),
            wo=np.ascontiguousarray(wo_[c * 256:(c + 1) * 256]),
            bq=np.ascontiguousarray(bq_[c * 256:(c + 1) * 256]),
            bk=np.ascontiguousarray(bk_[c * 256:(c + 1) * 256]),
            bv=np.ascontiguousarray(bv_[c * 256:(c + 1) * 256]),
            bo=bo_,
            wu=np.ascontiguousarray(wu_[c * 1024:(c + 1) * 1024]),
            wgt=np.ascontiguousarray(wg_[c * 1024:(c + 1) * 1024]),
            wd=np.ascontiguousarray(wd_[c * 256:(c + 1) * 256]),
        ))

    res = run_bass_kernel_spmd(nc, in_maps, core_ids=list(range(8)))
    outp = np.concatenate([res.results[c]["out"] for c in range(8)], axis=0)
    if DEBUG:
        kernel.last_results = res
    return outp.reshape(1, N, D)


# version 16 (final)
# speedup vs baseline: 1.0118x; 1.0118x over previous
import numpy as np
import ml_dtypes
from contextlib import ExitStack

import concourse.bass as bass
import concourse.mybir as mybir
from concourse.bass_utils import run_bass_kernel_spmd

F32 = mybir.dt.float32
BF16 = mybir.dt.bfloat16
U32 = mybir.dt.uint32
AF = mybir.ActivationFunctionType
ALU = mybir.AluOpType
AX = mybir.AxisListType.X
RG = [[0, 1, 2, 3, 4, 5, 6, 7]]

D = 2048
N = 8192
E = 8192
TOK = 1024
ISQ = float(1.0 / np.sqrt(128.0))
EPS_LN = 1e-5
EPS_Q = 1e-8
NEL_W = float(E * D)

DEBUG = False
HOIST_EH = 3


class Hist:
    def __init__(self, depth):
        self.depth = depth
        self.marks = []

    def guard(self):
        if len(self.marks) >= self.depth:
            return self.marks[-self.depth]
        return None

    def push(self, sem_name, value):
        self.marks.append((sem_name, value))


def _apkey(a):
    try:
        return (a.memref, a.offset, tuple(tuple(p) for p in a.ap), a.dtype)
    except Exception:
        return None


def _clone_noldw(inst, name_suffix="_noldw"):
    kw = dict(name=inst.name + name_suffix, ins=list(inst.ins), outs=list(inst.outs),
              start_tensor_calc=inst.start_tensor_calc,
              stop_tensor_calc=inst.stop_tensor_calc,
              replication_resolution=inst.replication_resolution,
              replication_shift_amnt=inst.replication_shift_amnt,
              replication_num_rows=inst.replication_num_rows,
              ldweights=False)
    for f in ("perf_mode", "is_transpose", "ifmap_quant_offset",
              "weights_quant_offset", "tile_position", "tile_size"):
        v = getattr(inst, f)
        if v is not None:
            kw[f] = v
    n = mybir.InstMatmult(**kw)
    n.sync_info = inst.sync_info
    n.engine = inst.engine
    return n


def dedupe_ldweights(nc):
    nrw = 0
    for b in nc.m.functions[0].blocks:
        prev_key = None
        insts = b.instructions
        for i in range(len(insts)):
            inst = insts[i]
            cn = inst.__class__.__name__
            if cn == "InstMatmult":
                if inst.is_transpose:
                    prev_key = None
                    continue
                k = _apkey(inst.ins[1])
                if (k is not None and k == prev_key
                        and inst.ins[1].dtype == BF16):
                    newi = _clone_noldw(inst)
                    nc.register_instruction(newi, overwrite=True)
                    insts[i] = newi
                    nrw += 1
                else:
                    prev_key = k
            elif getattr(inst, "is_sequencer_only", False):
                continue
            else:
                prev_key = None
    return nrw


def build_program():
    nc = bass.Bass("TRN2", target_bir_lowering=False, detect_race_conditions=False)

    xs = nc.dram_tensor("xs", [TOK, D], F32, kind="ExternalInput")
    wq = nc.dram_tensor("wq", [256, D], F32, kind="ExternalInput")
    wk = nc.dram_tensor("wk", [256, D], F32, kind="ExternalInput")
    wv = nc.dram_tensor("wv", [256, D], F32, kind="ExternalInput")
    wo = nc.dram_tensor("wo", [256, D], F32, kind="ExternalInput")
    bq = nc.dram_tensor("bq", [256, 1], F32, kind="ExternalInput")
    bk = nc.dram_tensor("bk", [256, 1], F32, kind="ExternalInput")
    bv = nc.dram_tensor("bv", [256, 1], F32, kind="ExternalInput")
    bo = nc.dram_tensor("bo", [1, D], F32, kind="ExternalInput")
    wu = nc.dram_tensor("wu", [1024, D], F32, kind="ExternalInput")
    wgt = nc.dram_tensor("wgt", [1024, D], F32, kind="ExternalInput")
    wd = nc.dram_tensor("wd", [256, E], F32, kind="ExternalInput")
    out = nc.dram_tensor("out", [TOK, D], F32, kind="ExternalOutput")
    pid_t = nc.partition_id_tensor

    xnh_in = nc.dram_tensor("xnh_in", [TOK, D], BF16, kind="Internal")
    xnl_in = nc.dram_tensor("xnl_in", [TOK, D], BF16, kind="Internal")
    xnhA = nc.dram_tensor("xnhA", [N // 2, D], BF16, kind="Internal")
    xnlA = nc.dram_tensor("xnlA", [N // 2, D], BF16, kind="Internal")
    xnhB = nc.dram_tensor("xnhB", [N // 2, D], BF16, kind="Internal")
    xnlB = nc.dram_tensor("xnlB", [N // 2, D], BF16, kind="Internal")
    wsp = {}
    for nm in ("wqh", "wql", "wkh", "wkl", "wvh", "wvl", "woh_in", "wol_in"):
        wsp[nm] = nc.dram_tensor(nm, [256, D], BF16, kind="Internal")
    woh = nc.dram_tensor("woh", [D, D], BF16, kind="Internal")
    wol = nc.dram_tensor("wol", [D, D], BF16, kind="Internal")
    scp_in = nc.dram_tensor("scp_in", [1, 16], F32, kind="Internal")
    scp = nc.dram_tensor("scp", [8, 16], F32, kind="Internal")
    scw = nc.dram_tensor("scw", [1, 16], F32, kind="Internal")
    tu_in = nc.dram_tensor("tu_in", [1024, D], BF16, kind="Internal")
    tg_in = nc.dram_tensor("tg_in", [1024, D], BF16, kind="Internal")
    td_in = nc.dram_tensor("td_in", [256, E], BF16, kind="Internal")
    tu = nc.dram_tensor("tu", [E, D], BF16, kind="Internal")
    tgg = nc.dram_tensor("tgg", [E, D], BF16, kind="Internal")
    td = nc.dram_tensor("td", [D, E], BF16, kind="Internal")
    ctxh_in = nc.dram_tensor("ctxh_in", [N, 256], BF16, kind="Internal")
    ctxl_in = nc.dram_tensor("ctxl_in", [N, 256], BF16, kind="Internal")
    ctxh = nc.dram_tensor("ctxh", [8 * N, 256], BF16, kind="Internal")
    ctxl = nc.dram_tensor("ctxl", [8 * N, 256], BF16, kind="Internal")
    myctxh = nc.dram_tensor("myctxh", [TOK, D], BF16, kind="Internal")
    myctxl = nc.dram_tensor("myctxl", [TOK, D], BF16, kind="Internal")
    yff = nc.dram_tensor("yff", [TOK, D], F32, kind="Internal")

    dbg = {}
    if DEBUG:
        for nm, shp, dt in [("dbg_xnh", [TOK, D], BF16), ("dbg_ctxh", [N, 256], BF16),
                            ("dbg_yff", [TOK, D], F32), ("dbg_scp", [8, 16], F32),
                            ("dbg_tu", [1024, D], BF16)]:
            dbg[nm] = nc.dram_tensor(nm, shp, dt, kind="ExternalOutput")

    ops = {k: [] for k in "tvsyg"}
    cnt = {k: 0 for k in ("PT", "PV", "PS", "PY", "PG")}
    sems = {}
    DMA_STREAMS = ("yc", "gpid", "x2a", "x2b", "hl0", "hl1", "gsc", "qsl", "qsw",
                   "qbc", "ywt", "xt0", "xt1", "ctx0", "ctx1", "grp",
                   "yct", "yctb", "yctc", "yctd",
                   "yfx", "yfxb", "yfxc", "yfxd", "td0", "td1",
                   "qw0", "qw1", "qts0", "qts1", "yf0", "yf1", "xt2", "xt3",
                   "td2", "td3", "td4", "td5",
                   "tu00", "tu01", "tu02", "tu03", "tu10", "tu11", "tu12", "tu13",
                   "tu20", "tu21", "tu22", "tu23",
                   "wo00", "wo01", "wo02", "wo03", "wo10", "wo11", "wo12", "wo13",
                   "xr0", "xr1", "ou0", "ou1")
    CC_STREAMS = ("ccxh", "ccxl", "ccxh2", "ccxl2", "ccwh", "ccwl", "ccsc",
                  "cctu", "cctg", "cctd", "ccch", "cccl")
    for k in DMA_STREAMS + CC_STREAMS:
        cnt[k] = 0

    def emit(e, fn):
        ops[e].append(fn)

    def inc(name, k=1):
        cnt[name] += k
        return cnt[name]

    with ExitStack() as ctx:
        for name in cnt:
            sems[name] = ctx.enter_context(nc.semaphore(name))
        S = sems

        def barrier():
            pm = {"t": "PT", "v": "PV", "s": "PS", "y": "PY", "g": "PG"}
            tg = {}
            for e in "tvsyg":
                tg[pm[e]] = inc(pm[e])
            snap = dict(tg)
            for e in "tvsyg":
                def f(eng, e=e, snap=snap, pm=pm):
                    eng.drain().then_inc(S[pm[e]], 1)
                    for o in "tvsyg":
                        if o != e:
                            eng.wait_ge(S[pm[o]], snap[pm[o]])
                emit(e, f)

        s_bqs = ctx.enter_context(nc.sbuf_tensor("s_bqs", [128, 2], F32))
        s_bks = ctx.enter_context(nc.sbuf_tensor("s_bks", [128, 2], F32))
        s_bvs = ctx.enter_context(nc.sbuf_tensor("s_bvs", [128, 2], F32))
        s_bob = ctx.enter_context(nc.sbuf_tensor("s_bob", [128, D], F32))
        s_mask = ctx.enter_context(nc.sbuf_tensor("s_mask", [128, 4 * 512], F32))
        s_ident = ctx.enter_context(nc.sbuf_tensor("s_ident", [128, 128], F32))
        s_ones_c = ctx.enter_context(nc.sbuf_tensor("s_ones_c", [128, 1], F32))
        s_ones_cb = ctx.enter_context(nc.sbuf_tensor("s_ones_cb", [128, 1], BF16))
        s_ones_r = ctx.enter_context(nc.sbuf_tensor("s_ones_r", [1, 128], F32))
        s_pid = ctx.enter_context(nc.sbuf_tensor("s_pid", [1, 1], U32))
        s_scales = ctx.enter_context(nc.sbuf_tensor("s_scales", [128, 3], F32))
        s_thr = ctx.enter_context(nc.sbuf_tensor("s_thr", [128, 6], F32))
        s_negmh = ctx.enter_context(nc.sbuf_tensor("s_negmh", [128, 1], F32))

        pidr = nc.gpsimd.alloc_register("pidr")
        pidoff = nc.gpsimd.alloc_register("pidoff")
        tmpr = nc.gpsimd.alloc_register("tmpr")

        pq = ExitStack()
        s_q2 = [pq.enter_context(nc.sbuf_tensor("s_q2_0", [128, D], F32))] * 2
        s_qq = pq.enter_context(nc.sbuf_tensor("s_qq", [128, D], BF16))
        s_qt = [pq.enter_context(nc.sbuf_tensor("s_qt_0", [128, D], BF16))] * 2
        s_qacc = pq.enter_context(nc.sbuf_tensor("s_qacc", [128, 4], F32))
        s_qabs = pq.enter_context(nc.sbuf_tensor("s_qabs", [128, 1], F32))
        s_qsc = pq.enter_context(nc.sbuf_tensor("s_qsc", [1, 16], F32))
        s_qall = pq.enter_context(nc.sbuf_tensor("s_qall", [1, 128], F32))

        p0 = ExitStack()
        s_x2 = [p0.enter_context(nc.sbuf_tensor(f"s_x2_{i}", [128, D], F32)) for i in range(2)]
        s_xn = p0.enter_context(nc.sbuf_tensor("s_xn", [128, D], F32))
        s_h2 = [p0.enter_context(nc.sbuf_tensor(f"s_h2_{i}", [128, D], BF16)) for i in range(2)]
        s_l2 = [p0.enter_context(nc.sbuf_tensor(f"s_l2_{i}", [128, D], BF16)) for i in range(2)]
        s_st = p0.enter_context(nc.sbuf_tensor("s_st", [128, 48], F32))
        s_sq0 = p0.enter_context(nc.sbuf_tensor("s_sq0", [128, D], F32))

        hx2 = Hist(2)
        hhl = Hist(2)
        hq2 = Hist(1)
        hqt = Hist(1)

        def f(y):
            y.dma_start(s_bqs[:], bass.AP(bq, 0, [[1, 128], [128, 2]]),
                        allow_slow_non_contiguous=True).then_inc(S["yc"], 16)
            y.dma_start(s_bks[:], bass.AP(bk, 0, [[1, 128], [128, 2]]),
                        allow_slow_non_contiguous=True).then_inc(S["yc"], 16)
            y.dma_start(s_bvs[:], bass.AP(bv, 0, [[1, 128], [128, 2]]),
                        allow_slow_non_contiguous=True).then_inc(S["yc"], 16)
            y.dma_start(s_bob[:], bass.AP(bo, 0, [[0, 128], [1, D]])).then_inc(S["yc"], 16)
        emit("y", f)
        CONST_LD = inc("yc", 64)

        def f(g):
            g.dma_start(s_pid[:], pid_t[:, :]).then_inc(S["gpid"], 16)
        emit("g", f)
        PID_LD = inc("gpid", 16)

        def f(g, PID_LD=PID_LD):
            g.wait_ge(S["gpid"], PID_LD)
            g.reg_load(pidr, s_pid[:1, :1])
            g.reg_mul(pidoff, pidr, TOK * 256)
            g.drain().then_inc(S["PG"], 1)
        emit("g", f)
        PIDREG_DONE = inc("PG")

        def f(v):
            for dlt in range(4):
                v.memset(s_mask[:, dlt * 512:(dlt + 1) * 512], 1.0)
            v.memset(s_ident[:], 1.0)
            v.drain().then_inc(S["PV"], 1)
        emit("v", f)
        MEMSET_DONE = inc("PV")

        def f(g, MEMSET_DONE=MEMSET_DONE):
            g.wait_ge(S["PV"], MEMSET_DONE)
            for dlt in range(4):
                g.affine_select(s_mask[:, dlt * 512:(dlt + 1) * 512],
                                s_mask[:, dlt * 512:(dlt + 1) * 512],
                                [[1, 512]], ALU.is_ge, 0.0,
                                base=-(128 * dlt),
                                channel_multiplier=-1).then_inc(S["PG"], 1)
            g.affine_select(s_ident[:], s_ident[:], [[1, 128]], ALU.is_equal, 0.0,
                            base=0, channel_multiplier=-1).then_inc(S["PG"], 1)
        emit("g", f)
        MASKS_DONE = inc("PG", 5)

        def f(v, CONST_LD=CONST_LD):
            v.memset(s_ones_c[:], 1.0)
            v.memset(s_ones_cb[:], 1.0)
            v.memset(s_ones_r[:], 1.0)
            v.memset(s_qacc[:], 0.0)
            v.memset(s_qsc[:], 0.0)
            v.wait_ge(S["yc"], CONST_LD)
            v.tensor_scalar_mul(s_bqs[:], s_bqs[:], ISQ)
            v.drain().then_inc(S["PV"], 1)
        emit("v", f)
        INIT_V = inc("PV")

        for t in range(8):
            gd = hx2.guard()
            def f(y, t=t, gd=gd):
                if gd:
                    y.wait_ge(S[gd[0]], gd[1])
                sl = "x2a" if t % 2 == 0 else "x2b"
                y.dma_start(s_x2[t % 2][:], xs[t * 128:(t + 1) * 128, :]).then_inc(S[sl], 16)
            emit("y", f)
            sl = "x2a" if t % 2 == 0 else "x2b"
            LDX = inc(sl, 16)

            def f(v, t=t, LDX=LDX, sl=sl):
                v.wait_ge(S[sl], LDX)
                xt = s_x2[t % 2]
                mu = s_st[:, t:t + 1]; ss = s_st[:, 8 + t:9 + t]
                var = s_st[:, 16 + t:17 + t]; ms = s_st[:, 24 + t:25 + t]
                v.reduce_sum(mu, xt[:], axis=AX)
                v.tensor_mul(s_sq0[:], xt[:], xt[:])
                v.drain()
                v.reduce_sum(ss, s_sq0[:], axis=AX)
                v.tensor_scalar_mul(mu, mu, 1.0 / D)
                v.drain()
                v.tensor_mul(ms, mu, mu)
                v.drain()
                v.scalar_tensor_tensor(var, ss, 1.0 / D, ms, ALU.mult, ALU.subtract)
                v.drain()
                v.tensor_scalar_add(var, var, EPS_LN)
                v.drain().then_inc(S["PV"], 1)
            emit("v", f)
            VARD = inc("PV")

            def f(s, t=t, VARD=VARD):
                s.wait_ge(S["PV"], VARD)
                s.activation(s_st[:, 24 + t:25 + t], s_st[:, 16 + t:17 + t], AF.Sqrt)
                s.drain().then_inc(S["PS"], 1)
            emit("s", f)
            SDD = inc("PS")

            gd = hhl.guard()
            def f(v, t=t, SDD=SDD, gd=gd):
                v.wait_ge(S["PS"], SDD)
                if gd:
                    v.wait_ge(S[gd[0]], gd[1])
                xt = s_x2[t % 2]
                mu = s_st[:, t:t + 1]; sd = s_st[:, 24 + t:25 + t]
                rstd = s_st[:, 32 + t:33 + t]; nm = s_st[:, 40 + t:41 + t]
                v.reciprocal(rstd, sd)
                v.drain()
                v.scalar_tensor_tensor(nm, mu, -1.0, rstd, ALU.mult, ALU.mult)
                v.drain()
                v.tensor_scalar(s_xn[:], xt[:], rstd, nm, ALU.mult, ALU.add)
                v.drain()
                v.tensor_copy(s_h2[t % 2][:], s_xn[:])
                v.drain()
                v.scalar_tensor_tensor(s_l2[t % 2][:], s_xn[:], 1.0, s_h2[t % 2][:],
                                       ALU.mult, ALU.subtract)
                v.drain().then_inc(S["PV"], 1)
            emit("v", f)
            XNS = inc("PV")
            hx2.push("PV", XNS)

            def f(g, t=t, XNS=XNS):
                g.wait_ge(S["PV"], XNS)
                hs = "hl0" if t % 2 == 0 else "hl1"
                g.dma_start(xnh_in[t * 128:(t + 1) * 128, :], s_h2[t % 2][:]).then_inc(S[hs], 16)
                g.dma_start(xnl_in[t * 128:(t + 1) * 128, :], s_l2[t % 2][:]).then_inc(S[hs], 16)
            emit("g", f)
            hs = "hl0" if t % 2 == 0 else "hl1"
            STX = inc(hs, 32)
            hhl.push(hs, STX)

        XN_ST0 = cnt["hl0"]
        XN_ST1 = cnt["hl1"]

        def f(g, XN_ST0=XN_ST0, XN_ST1=XN_ST1):
            g.wait_ge(S["hl0"], XN_ST0)
            g.wait_ge(S["hl1"], XN_ST1)
            g.collective_compute("AllGather", ALU.bypass, RG,
                                 [xnh_in[0:512, :].opt()], [xnhA.ap().opt()]).then_inc(S["ccxh"])
            g.collective_compute("AllGather", ALU.bypass, RG,
                                 [xnl_in[0:512, :].opt()], [xnlA.ap().opt()]).then_inc(S["ccxl"])
            g.collective_compute("AllGather", ALU.bypass, RG,
                                 [xnh_in[512:1024, :].opt()], [xnhB.ap().opt()]).then_inc(S["ccxh2"])
            g.collective_compute("AllGather", ALU.bypass, RG,
                                 [xnl_in[512:1024, :].opt()], [xnlB.ap().opt()]).then_inc(S["ccxl2"])
        emit("g", f)
        CC_XNH = inc("ccxh")
        CC_XNL = inc("ccxl")
        CC_XNH2 = inc("ccxh2")
        CC_XNL2 = inc("ccxl2")
        XCHUNKS = [(xnhA, xnlA, i, i * 1024, "ccxh", "ccxl", CC_XNH, CC_XNL)
                   for i in range(8)]
        XCHUNKS += [(xnhB, xnlB, i, i * 1024 + 512, "ccxh2", "ccxl2", CC_XNH2, CC_XNL2)
                    for i in range(8)]

        for wi, (wsrc, hdst, ldst) in enumerate([
                (wq, wsp["wqh"], wsp["wql"]), (wk, wsp["wkh"], wsp["wkl"]),
                (wv, wsp["wvh"], wsp["wvl"]), (wo, wsp["woh_in"], wsp["wol_in"])]):
            for j in range(2):
                gd = hx2.guard()
                def f(y, wsrc=wsrc, j=j, gd=gd):
                    if gd:
                        y.wait_ge(S[gd[0]], gd[1])
                    sl = "x2a" if j == 0 else "x2b"
                    y.dma_start(s_x2[j][:], wsrc[j * 128:(j + 1) * 128, :]).then_inc(S[sl], 16)
                emit("y", f)
                sl = "x2a" if j == 0 else "x2b"
                WLD = inc(sl, 16)

                gd = hhl.guard()
                def f(v, j=j, WLD=WLD, gd=gd, sl=sl):
                    v.wait_ge(S[sl], WLD)
                    if gd:
                        v.wait_ge(S[gd[0]], gd[1])
                    v.tensor_copy(s_h2[j][:], s_x2[j][:])
                    v.drain()
                    v.scalar_tensor_tensor(s_l2[j][:], s_x2[j][:], 1.0, s_h2[j][:],
                                           ALU.mult, ALU.subtract)
                    v.drain().then_inc(S["PV"], 1)
                emit("v", f)
                WSPv = inc("PV")
                hx2.push("PV", WSPv)

                def f(g, hdst=hdst, ldst=ldst, j=j, WSPv=WSPv):
                    g.wait_ge(S["PV"], WSPv)
                    hs = "hl0" if j == 0 else "hl1"
                    g.dma_start(hdst[j * 128:(j + 1) * 128, :], s_h2[j][:]).then_inc(S[hs], 16)
                    g.dma_start(ldst[j * 128:(j + 1) * 128, :], s_l2[j][:]).then_inc(S[hs], 16)
                emit("g", f)
                hs = "hl0" if j == 0 else "hl1"
                WST = inc(hs, 32)
                hhl.push(hs, WST)

        WSPLIT_ST0 = cnt["hl0"]
        WSPLIT_ST1 = cnt["hl1"]

        def f(g, WSPLIT_ST0=WSPLIT_ST0, WSPLIT_ST1=WSPLIT_ST1):
            g.wait_ge(S["hl0"], WSPLIT_ST0)
            g.wait_ge(S["hl1"], WSPLIT_ST1)
            g.collective_compute("AllGather", ALU.bypass, RG,
                                 [wsp["woh_in"].ap().opt()], [woh.ap().opt()]).then_inc(S["ccwh"])
            g.collective_compute("AllGather", ALU.bypass, RG,
                                 [wsp["wol_in"].ap().opt()], [wol.ap().opt()]).then_inc(S["ccwl"])
        emit("g", f)
        CC_WOH = inc("ccwh")
        CC_WOL = inc("ccwl")

        def f(v, WSPLIT_ST0=WSPLIT_ST0, WSPLIT_ST1=WSPLIT_ST1):
            v.wait_ge(S["hl0"], WSPLIT_ST0)
            v.wait_ge(S["hl1"], WSPLIT_ST1)
            v.memset(s_vx[:], 1.0)
            v.memset(s_mx[:, 0:2], 0.0)
            v.drain().then_inc(S["PV"], 1)
        emit("v", f)
        VX_INIT0 = inc("PV")

        WT = [(wu, (i * 128, 0)) for i in range(8)]
        WT += [(wgt, (i * 128, 0)) for i in range(8)]
        WT += [(wd, ((i // 4) * 128, (i % 4) * 2048)) for i in range(8)]
        TDST = [(tu_in, (i * 128, 0)) for i in range(8)]
        TDST += [(tg_in, (i * 128, 0)) for i in range(8)]
        TDST += [(td_in, ((i // 4) * 128, (i % 4) * 2048)) for i in range(8)]

        for k, (wsrc, (r0, c0)) in enumerate(WT):
            widx = k // 8
            gd = hq2.guard()
            qsl_ = "qw0"
            def f(y, wsrc=wsrc, r0=r0, c0=c0, k=k, gd=gd, qsl_=qsl_):
                if gd:
                    y.wait_ge(S[gd[0]], gd[1])
                y.dma_start(s_q2[k % 2][:], wsrc[r0:r0 + 128, c0:c0 + D]).then_inc(S[qsl_], 16)
            emit("y", f)
            WLD = inc(qsl_, 16)

            def f(v, k=k, widx=widx, WLD=WLD, qsl_=qsl_):
                v.wait_ge(S[qsl_], WLD)
                v.reduce_sum(s_qabs[:], s_q2[k % 2][:], axis=AX, apply_absolute_value=True)
                v.drain()
                v.tensor_add(s_qacc[:, widx:widx + 1], s_qacc[:, widx:widx + 1],
                             s_qabs[:])
                v.drain().then_inc(S["PV"], 1)
            emit("v", f)
            SUMV = inc("PV")
            hq2.push("PV", SUMV)

        SUM_DONE = cnt["PV"]

        p0.close()

        p1 = ExitStack()
        s_qh = p1.enter_context(nc.sbuf_tensor("s_qh", [128, N], BF16))
        s_ql = p1.enter_context(nc.sbuf_tensor("s_ql", [128, N], BF16))
        s_kh = p1.enter_context(nc.sbuf_tensor("s_kh", [128, N], BF16))
        s_kl = p1.enter_context(nc.sbuf_tensor("s_kl", [128, N], BF16))
        s_vtf = p1.enter_context(nc.sbuf_tensor("s_vtf", [128, N], F32))
        s_vx = p1.enter_context(nc.sbuf_tensor("s_vx", [128, 64 * 132], F32))
        s_wqt = [p1.enter_context(nc.sbuf_tensor(f"s_wqt{i}", [128, D], BF16,
                                                 is_transpose_dest=True)) for i in range(2)]
        s_wkt = [p1.enter_context(nc.sbuf_tensor(f"s_wkt{i}", [128, D], BF16,
                                                 is_transpose_dest=True)) for i in range(2)]
        s_wvt = [p1.enter_context(nc.sbuf_tensor(f"s_wvt{i}", [128, D], BF16,
                                                 is_transpose_dest=True)) for i in range(2)]
        s_xth = [p1.enter_context(nc.sbuf_tensor(f"s_xth{i}", [128, 512], BF16,
                                                 is_transpose_dest=True)) for i in range(4)]
        s_xtl = [p1.enter_context(nc.sbuf_tensor(f"s_xtl{i}", [128, 512], BF16,
                                                 is_transpose_dest=True)) for i in range(4)]
        s_ptmp = p1.enter_context(nc.sbuf_tensor("s_ptmp", [128, 512], F32))
        s_ptmp2 = p1.enter_context(nc.sbuf_tensor("s_ptmp2", [128, 512], F32))
        s_sqb = p1.enter_context(nc.sbuf_tensor("s_sqb", [128, 512], BF16))
        s_mx = p1.enter_context(nc.sbuf_tensor("s_mx", [1, 8], F32))
        s_pt = [p1.enter_context(nc.sbuf_tensor(f"s_pt{i}", [128, 512], F32)) for i in range(3)]
        s_ctxf = p1.enter_context(nc.sbuf_tensor("s_ctxf", [128, 128], F32))
        s_rz = p1.enter_context(nc.sbuf_tensor("s_rz", [128, 1], F32))
        s_cth = [p1.enter_context(nc.sbuf_tensor(f"s_cth{i}", [128, 128], BF16)) for i in range(2)]
        s_ctl = [p1.enter_context(nc.sbuf_tensor(f"s_ctl{i}", [128, 128], BF16)) for i in range(2)]

        hxt = Hist(4)
        hwt = Hist(1)
        hps = Hist(3)
        hpt = Hist(2)
        hctx = Hist(2)
        it_idx = [0]

        state = {"SCP_RDY": 0, "psA_extra_wait": 0, "QSL_LD": 0, "SCALES_V": 0,
                 "THR_BC": 0, "quant_next": 0, "QT_ST": {}, "psNX": None}

        def slot_psA(KMX_cur):
            def f(t, SUM_DONE=SUM_DONE, KMX_cur=KMX_cur):
                t.wait_ge(S["PV"], SUM_DONE)
                t.wait_ge(S["PV"], KMX_cur)
                t.matmul(state["psNX"][0:1, 0:3], s_ones_c[:], s_qacc[:, 0:3],
                         start=True, stop=True)
                t.drain().then_inc(S["PT"], 1)
            emit("t", f)
            PSA_T = inc("PT")

            def f(v, PSA_T=PSA_T):
                v.wait_ge(S["PT"], PSA_T)
                v.tensor_copy(s_qsc[:, 0:3], state["psNX"][0:1, 0:3])
                v.drain().then_inc(S["PV"], 1)
            emit("v", f)
            state["SCP_RDY"] = inc("PV")
            state["psA_extra_wait"] = state["SCP_RDY"]

            def f(g, M=state["SCP_RDY"]):
                g.wait_ge(S["PV"], M)
                g.dma_start(scp_in[:, :], s_qsc[:]).then_inc(S["gsc"], 16)
            emit("g", f)
            SCP_ST = inc("gsc", 16)

            def f(g, SCP_ST=SCP_ST):
                g.wait_ge(S["gsc"], SCP_ST)
                g.collective_compute("AllGather", ALU.bypass, RG,
                                     [scp_in.ap().opt()], [scp.ap().opt()]
                                     ).then_inc(S["ccsc"])
            emit("g", f)
            inc("ccsc")

        def slot_qsl():
            def f(g, CC=cnt["ccsc"]):
                g.wait_ge(S["ccsc"], CC)
                g.dma_start(s_qall[:], bass.AP(scp, 0, [[128, 1], [1, 128]])
                            ).then_inc(S["qsl"], 16)
            emit("g", f)
            state["QSL_LD"] = inc("qsl", 16)

        def slot_scales():
            def f(v, QSL_LD=state["QSL_LD"]):
                v.wait_ge(S["qsl"], QSL_LD)
                for w in range(3):
                    v.reduce_sum(s_qsc[:, w:w + 1],
                                 bass.AP(s_qall, w, [[128, 1], [16, 8]]), axis=AX)
                    v.drain()
                    v.tensor_scalar(s_qsc[:, 3 + w:4 + w], s_qsc[:, w:w + 1],
                                    1.0 / NEL_W, EPS_Q, ALU.mult, ALU.add)
                    v.drain()
                    v.tensor_scalar_mul(s_qsc[:, 6 + w:7 + w], s_qsc[:, 3 + w:4 + w], 0.5)
                    v.tensor_scalar_mul(s_qsc[:, 9 + w:10 + w], s_qsc[:, 3 + w:4 + w], -0.5)
                    v.drain()
                v.drain().then_inc(S["PV"], 1)
            emit("v", f)
            state["SCALES_V"] = inc("PV")

            def f(g, M=state["SCALES_V"]):
                g.wait_ge(S["PV"], M)
                g.dma_start(scw[:, :], s_qsc[:]).then_inc(S["qsw"], 16)
            emit("g", f)
            SCW_ST = inc("qsw", 16)

            def f(g, SCW_ST=SCW_ST):
                g.wait_ge(S["qsw"], SCW_ST)
                g.dma_start(s_scales[:], bass.AP(scw, 3, [[0, 128], [1, 3]])
                            ).then_inc(S["qbc"], 16)
                g.dma_start(s_thr[:], bass.AP(scw, 6, [[0, 128], [1, 6]])
                            ).then_inc(S["qbc"], 16)
            emit("g", f)
            state["THR_BC"] = inc("qbc", 32)

        def slot_quant_tile():
            k = state["quant_next"]
            if k >= len(WT):
                return
            state["quant_next"] = k + 1
            wsrc, (r0, c0) = WT[k]
            tdst, (tr0, tc0) = TDST[k]
            widx = k // 8
            qsl_ = "qw0"
            gd = hq2.guard()
            def f(y, wsrc=wsrc, r0=r0, c0=c0, k=k, gd=gd, qsl_=qsl_):
                if gd:
                    y.wait_ge(S[gd[0]], gd[1])
                y.dma_start(s_q2[k % 2][:], wsrc[r0:r0 + 128, c0:c0 + D]
                            ).then_inc(S[qsl_], 16)
            emit("y", f)
            WLD = inc(qsl_, 16)

            gq = hqt.guard()
            def f(v, k=k, widx=widx, WLD=WLD, qsl_=qsl_, gq=gq, THR=state["THR_BC"]):
                v.wait_ge(S[qsl_], WLD)
                v.wait_ge(S["qbc"], THR)
                if gq:
                    v.wait_ge(S[gq[0]], gq[1])
                v.tensor_single_scalar(s_qq[:], s_q2[k % 2][:],
                                       s_thr[:, 3 + widx:4 + widx], ALU.is_lt)
                v.drain()
                v.scalar_tensor_tensor(s_qt[k % 2][:], s_q2[k % 2][:],
                                       s_thr[:, widx:widx + 1], s_qq[:],
                                       ALU.is_gt, ALU.subtract)
                v.drain().then_inc(S["PV"], 1)
            emit("v", f)
            QV = inc("PV")
            hq2.push("PV", QV)

            tsl = "qts0"
            def f(g, tdst=tdst, tr0=tr0, tc0=tc0, k=k, QV=QV, tsl=tsl):
                g.wait_ge(S["PV"], QV)
                g.dma_start(tdst[tr0:tr0 + 128, tc0:tc0 + D],
                            s_qt[k % 2][:]).then_inc(S[tsl], 16)
            emit("g", f)
            QST = inc(tsl, 16)
            hqt.push(tsl, QST)
            state["QT_ST"][k] = (tsl, QST)

        def slot_t_ags():
            q0 = cnt["qts0"]
            q1 = cnt["qts1"]
            def f(g, q0=q0, q1=q1):
                g.wait_ge(S["qts0"], q0)
                g.wait_ge(S["qts1"], q1)
                g.collective_compute("AllGather", ALU.bypass, RG,
                                     [tu_in.ap().opt()], [tu.ap().opt()]).then_inc(S["cctu"])
                g.collective_compute("AllGather", ALU.bypass, RG,
                                     [tg_in.ap().opt()], [tgg.ap().opt()]).then_inc(S["cctg"])
                g.collective_compute("AllGather", ALU.bypass, RG,
                                     [td_in.ap().opt()], [td.ap().opt()]).then_inc(S["cctd"])
            emit("g", f)
            inc("cctu"); inc("cctg"); inc("cctd")

        for h in range(2):
            gdw = hwt.guard()
            def f(y, h=h, gdw=gdw, WSPLIT_ST0=WSPLIT_ST0, WSPLIT_ST1=WSPLIT_ST1):
                y.wait_ge(S["hl0"], WSPLIT_ST0)
                y.wait_ge(S["hl1"], WSPLIT_ST1)
                if gdw:
                    y.wait_ge(S[gdw[0]], gdw[1])
                for dc in range(16):
                    for bf, big in ((0, wsp["wqh"]), (1, wsp["wql"])):
                        y.dma_start(s_wqt[bf][:, dc * 128:(dc + 1) * 128],
                                    big[h * 128:(h + 1) * 128, dc * 128:(dc + 1) * 128],
                                    transpose=True).then_inc(S["ywt"], 16)
                    for bf, big in ((0, wsp["wkh"]), (1, wsp["wkl"])):
                        y.dma_start(s_wkt[bf][:, dc * 128:(dc + 1) * 128],
                                    big[h * 128:(h + 1) * 128, dc * 128:(dc + 1) * 128],
                                    transpose=True).then_inc(S["ywt"], 16)
                    for bf, big in ((0, wsp["wvh"]), (1, wsp["wvl"])):
                        y.dma_start(s_wvt[bf][:, dc * 128:(dc + 1) * 128],
                                    big[h * 128:(h + 1) * 128, dc * 128:(dc + 1) * 128],
                                    transpose=True).then_inc(S["ywt"], 16)
            emit("y", f)
            WT_LD = inc("ywt", 16 * 96)

            if h == 0:
                VX_INIT = VX_INIT0
            else:
                def f(v, h=h):
                    v.memset(s_vx[:], 1.0)
                    v.memset(s_mx[:, 0:2], 0.0)
                    v.drain().then_inc(S["PV"], 1)
                emit("v", f)
                VX_INIT = inc("PV")

            pp = ExitStack()
            psQ = pp.enter_context(nc.psum_tensor(f"psQ_h{h}", [128, 512], F32))
            psK = pp.enter_context(nc.psum_tensor(f"psK_h{h}", [128, 512], F32))
            psV1 = pp.enter_context(nc.psum_tensor(f"psV1_h{h}", [128, 512], F32))
            psNX = pp.enter_context(nc.psum_tensor(f"psNX_h{h}", [128, 512], F32))
            psB = pp.enter_context(nc.psum_tensor(f"psB_h{h}", [128, 1], F32))
            psT = [pp.enter_context(nc.psum_tensor(f"psT{i}_h{h}", [128, 128], F32))
                   for i in range(2)]
            state["psNX"] = psNX

            SEVAC_prev = 0
            EVAC_prev = 0
            KSQ_prev = 0
            NRM_prev = 0
            KMX_prev = 0
            for sc in range(16):
                srcH, srcL, rb, g0, ccH, ccL, vH, vL = XCHUNKS[sc]
                dc_marks_now = [cnt["PT"] + 1 + i for i in range(16)]
                dy_dc = []
                for dc in range(16):
                    gd = hxt.guard()
                    xsl = ("xt0", "xt1", "xt2", "xt3")[dc % 4]
                    def f(y, sc=sc, dc=dc, gd=gd, WT_LD=WT_LD, xsl=xsl,
                          srcH=srcH, srcL=srcL, rb=rb, ccH=ccH, ccL=ccL,
                          vH=vH, vL=vL):
                        if sc == 0 and dc == 0:
                            y.wait_ge(S["ywt"], WT_LD)
                        if dc == 0:
                            y.wait_ge(S[ccH], vH)
                            y.wait_ge(S[ccL], vL)
                        if gd:
                            y.wait_ge(S[gd[0]], gd[1])
                        y.dma_start(s_xth[dc % 4][:],
                                    srcH[rb * 512:(rb + 1) * 512, dc * 128:(dc + 1) * 128],
                                    transpose=True).then_inc(S[xsl], 16)
                        y.dma_start(s_xtl[dc % 4][:],
                                    srcL[rb * 512:(rb + 1) * 512, dc * 128:(dc + 1) * 128],
                                    transpose=True).then_inc(S[xsl], 16)
                    emit("y", f)
                    dy_dc.append((xsl, inc(xsl, 32)))
                    hxt.push("PT", dc_marks_now[dc])

                def f(t, sc=sc, dy_dc=tuple(dy_dc), VX_INIT=VX_INIT,
                      SEVAC_prev=SEVAC_prev, EVAC_prev=EVAC_prev):
                    t.wait_ge(S["PV"], VX_INIT)
                    if SEVAC_prev:
                        t.wait_ge(S["PS"], SEVAC_prev)
                    for dc in range(16):
                        t.wait_ge(S[dy_dc[dc][0]], dy_dc[dc][1])
                        st0 = (dc == 0)
                        sp = (dc == 15)
                        wq0 = s_wqt[0][:, dc * 128:(dc + 1) * 128]
                        wq1 = s_wqt[1][:, dc * 128:(dc + 1) * 128]
                        wk0 = s_wkt[0][:, dc * 128:(dc + 1) * 128]
                        wk1 = s_wkt[1][:, dc * 128:(dc + 1) * 128]
                        wv0 = s_wvt[0][:, dc * 128:(dc + 1) * 128]
                        wv1 = s_wvt[1][:, dc * 128:(dc + 1) * 128]
                        xh = s_xth[dc % 4]; xl = s_xtl[dc % 4]
                        t.matmul(psQ[:], wq0, xh[:], start=st0, stop=False)
                        t.matmul(psQ[:], wq0, xl[:], start=False, stop=False)
                        t.matmul(psQ[:], wq1, xh[:], start=False, stop=sp)
                        t.matmul(psK[:], wk0, xh[:], start=st0, stop=False)
                        t.matmul(psK[:], wk0, xl[:], start=False, stop=False)
                        t.matmul(psK[:], wk1, xh[:], start=False, stop=sp)
                        t.matmul(psV1[:], wv0, xh[:], start=st0, stop=False)
                        t.matmul(psV1[:], wv0, xl[:], start=False, stop=False)
                        mm = t.matmul(psV1[:], wv1, xh[:], start=False, stop=sp)
                        mm.then_inc(S["PT"], 1)
                    t.drain().then_inc(S["PT"], 1)
                emit("t", f)
                dc_marks = [inc("PT") for _ in range(16)]
                assert dc_marks == dc_marks_now
                PROJ_T = inc("PT")

                def f(s, g0=g0, h=h, PROJ_T=PROJ_T, KSQ_prev=KSQ_prev, INIT_V=INIT_V):
                    s.wait_ge(S["PT"], PROJ_T)
                    s.wait_ge(S["PV"], INIT_V)
                    if KSQ_prev:
                        s.wait_ge(S["PV"], KSQ_prev)
                    s.activation(s_ptmp[:], psQ[:], AF.Identity, bias=s_bqs[:, h:h + 1],
                                 scale=ISQ)
                    s.activation(s_ptmp2[:], psK[:], AF.Identity, bias=s_bks[:, h:h + 1])
                    s.activation(s_vtf[:, g0:g0 + 512], psV1[:], AF.Identity,
                                 bias=s_bvs[:, h:h + 1])
                    s.drain().then_inc(S["PS"], 1)
                emit("s", f)
                SEVAC = inc("PS")

                def f(v, g0=g0, h=h, SEVAC=SEVAC, NRM_prev=NRM_prev):
                    v.wait_ge(S["PS"], SEVAC)
                    if NRM_prev:
                        v.wait_ge(S["PT"], NRM_prev)
                    c0 = g0
                    v.tensor_copy(s_qh[:, c0:c0 + 512], s_ptmp[:])
                    v.tensor_copy(s_kh[:, c0:c0 + 512], s_ptmp2[:])
                    v.drain()
                    v.scalar_tensor_tensor(s_ql[:, c0:c0 + 512], s_ptmp[:], 1.0,
                                           s_qh[:, c0:c0 + 512], ALU.mult, ALU.subtract)
                    v.scalar_tensor_tensor(s_kl[:, c0:c0 + 512], s_ptmp2[:], 1.0,
                                           s_kh[:, c0:c0 + 512], ALU.mult, ALU.subtract)
                    v.tensor_mul(s_sqb[:], s_ptmp[:], s_ptmp[:])
                    v.drain().then_inc(S["PV"], 1)
                emit("v", f)
                EVAC = inc("PV")

                def f(t, EVAC=EVAC, KMX_prev=KMX_prev,
                      XNQ=state["psA_extra_wait"] if sc == 3 and h == 0 else 0):
                    t.wait_ge(S["PV"], EVAC)
                    if KMX_prev:
                        t.wait_ge(S["PV"], KMX_prev)
                    if XNQ:
                        t.wait_ge(S["PV"], XNQ)
                    t.matmul(psNX[0:1, :], s_ones_cb[:], s_sqb[:], start=True, stop=True)
                    t.drain().then_inc(S["PT"], 1)
                emit("t", f)
                NQ_T = inc("PT")

                def f(v, NQ_T=NQ_T):
                    v.wait_ge(S["PT"], NQ_T)
                    v.reduce_max(s_mx[:, 2:3], psNX[0:1, :], axis=AX)
                    v.drain()
                    v.tensor_max(s_mx[:, 0:1], s_mx[:, 0:1], s_mx[:, 2:3])
                    v.tensor_mul(s_sqb[:], s_ptmp2[:], s_ptmp2[:])
                    v.drain().then_inc(S["PV"], 1)
                emit("v", f)
                KSQ = inc("PV")

                def f(t, KSQ=KSQ):
                    t.wait_ge(S["PV"], KSQ)
                    t.matmul(psNX[0:1, :], s_ones_cb[:], s_sqb[:], start=True, stop=True)
                    t.drain().then_inc(S["PT"], 1)
                emit("t", f)
                NRM = inc("PT")

                def f(v, NRM=NRM):
                    v.wait_ge(S["PT"], NRM)
                    v.reduce_max(s_mx[:, 2:3], psNX[0:1, :], axis=AX)
                    v.drain()
                    v.tensor_max(s_mx[:, 1:2], s_mx[:, 1:2], s_mx[:, 2:3])
                    v.drain().then_inc(S["PV"], 1)
                emit("v", f)
                KMX = inc("PV")

                SEVAC_prev, EVAC_prev, KSQ_prev, NRM_prev, KMX_prev = \
                    SEVAC, EVAC, KSQ, NRM, KMX

                if h == 0:
                    if sc == 2:
                        slot_psA(KMX)
                    elif sc == 3:
                        slot_qsl()
                    elif sc == 4:
                        slot_scales()

            htp = Hist(2)
            for kt in range(64):
                gtp = htp.guard()
                def f(t, kt=kt, gtp=gtp, SEVAC_prev=SEVAC_prev, VX=VX_INIT,
                      MD=MASKS_DONE):
                    if kt == 0:
                        t.wait_ge(S["PS"], SEVAC_prev)
                        t.wait_ge(S["PG"], MD)
                    if gtp:
                        t.wait_ge(S[gtp[0]], gtp[1])
                    t.transpose(psT[kt % 2][:], s_vtf[:, kt * 128:(kt + 1) * 128],
                                s_ident[:])
                    t.drain().then_inc(S["PT"], 1)
                emit("t", f)
                TPD = inc("PT")

                def f(v, kt=kt, TPD=TPD, VX=VX_INIT):
                    v.wait_ge(S["PT"], TPD)
                    v.wait_ge(S["PV"], VX)
                    v.tensor_copy(s_vx[:, kt * 132:kt * 132 + 128], psT[kt % 2][:])
                    v.drain().then_inc(S["PV"], 1)
                emit("v", f)
                TPV = inc("PV")
                htp.push("PV", TPV)

            def f(v, KMX_prev=KMX_prev):
                v.wait_ge(S["PV"], KMX_prev)
                v.tensor_mul(s_mx[:, 3:4], s_mx[:, 0:1], s_mx[:, 1:2])
                v.drain().then_inc(S["PV"], 1)
            emit("v", f)
            MH1 = inc("PV")

            def f(s, MH1=MH1):
                s.wait_ge(S["PV"], MH1)
                s.activation(s_mx[:, 4:5], s_mx[:, 3:4], AF.Sqrt)
                s.drain().then_inc(S["PS"], 1)
            emit("s", f)
            MH2 = inc("PS")

            def f(v, MH2=MH2):
                v.wait_ge(S["PS"], MH2)
                v.tensor_scalar_mul(s_mx[:, 5:6], s_mx[:, 4:5], -1.01)
                v.drain().then_inc(S["PV"], 1)
            emit("v", f)
            MH3 = inc("PV")

            def f(t, MH3=MH3):
                t.wait_ge(S["PV"], MH3)
                t.matmul(psB[:], s_ones_r[:], s_mx[:, 5:6], start=True, stop=True)
                t.drain().then_inc(S["PT"], 1)
            emit("t", f)
            MH4 = inc("PT")

            def f(v, MH4=MH4):
                v.wait_ge(S["PT"], MH4)
                v.tensor_copy(s_negmh[:], psB[:])
                v.drain().then_inc(S["PV"], 1)
            emit("v", f)
            MH_RDY = inc("PV")

            hwt.push("PT", PROJ_T)
            pp.close()

            pa = ExitStack()
            psS = [pa.enter_context(nc.psum_tensor(f"psS{i}_h{h}", [128, 512], F32))
                   for i in range(3)]
            psC = [pa.enter_context(nc.psum_tensor(f"psC{i}_h{h}", [128, 129], F32))
                   for i in range(4)]

            CTXV_prev_qc = [0, 0, 0, 0]
            for qc in range(16):
                nkt = 4 * qc + 4
                pending_pv = None
                for kt in range(nkt):
                    it = it_idx[0]
                    it_idx[0] += 1
                    bS = it % 3
                    gps = hps.guard()
                    def f(t, qc=qc, kt=kt, bS=bS, gps=gps, MH_RDY=MH_RDY,
                          CV=tuple(CTXV_prev_qc)):
                        if kt == 0:
                            t.wait_ge(S["PV"], MH_RDY)
                            for cv in CV:
                                if cv:
                                    t.wait_ge(S["PV"], cv)
                        if gps:
                            t.wait_ge(S[gps[0]], gps[1])
                        q0 = qc * 512
                        kc = slice(kt * 128, (kt + 1) * 128)
                        t.matmul(psS[bS][:], s_kh[:, kc], s_qh[:, q0:q0 + 512],
                                 start=True, stop=False)
                        t.matmul(psS[bS][:], s_kh[:, kc], s_ql[:, q0:q0 + 512],
                                 start=False, stop=False)
                        t.matmul(psS[bS][:], s_kl[:, kc], s_qh[:, q0:q0 + 512],
                                 start=False, stop=True).then_inc(S["PT"], 1)
                    emit("t", f)
                    SCO = inc("PT")

                    gpt = hpt.guard()
                    def f(s, bS=bS, SCO=SCO, gpt=gpt):
                        s.wait_ge(S["PT"], SCO)
                        if gpt:
                            s.wait_ge(S[gpt[0]], gpt[1])
                        s.activation(s_pt[bS][:], psS[bS][:], AF.Exp,
                                     bias=s_negmh[:, 0:1]).then_inc(S["PS"], 1)
                    emit("s", f)
                    EXPM = inc("PS")
                    hps.push("PS", EXPM)

                    MSK = None
                    if kt >= 4 * qc:
                        dlt = kt - 4 * qc
                        def f(v, bS=bS, dlt=dlt, EXPM=EXPM, MASKS_DONE=MASKS_DONE):
                            v.wait_ge(S["PS"], EXPM)
                            v.wait_ge(S["PG"], MASKS_DONE)
                            v.tensor_mul(s_pt[bS][:], s_pt[bS][:],
                                         s_mask[:, dlt * 512:(dlt + 1) * 512]
                                         ).then_inc(S["PV"], 1)
                        emit("v", f)
                        MSK = inc("PV")

                    def fpv(t, qc=qc, kt=kt, bS=bS, nkt=nkt, EXPM=EXPM, MSK=MSK):
                        t.wait_ge(S["PS"], EXPM)
                        if MSK is not None:
                            t.wait_ge(S["PV"], MSK)
                        for sub in range(4):
                            mm = t.matmul(psC[sub][:],
                                          s_pt[bS][:, sub * 128:(sub + 1) * 128],
                                          s_vx[:, kt * 132:kt * 132 + 129],
                                          start=(kt == 0), stop=(kt == nkt - 1))
                        mm.then_inc(S["PT"], 1)
                    if pending_pv is not None:
                        emit("t", pending_pv)
                        PVDp = inc("PT")
                        hpt.push("PT", PVDp)
                    pending_pv = fpv
                emit("t", pending_pv)
                PVDl = inc("PT")
                hpt.push("PT", PVDl)

                LASTPV = cnt["PT"]
                for sub in range(4):
                    rot = sub % 2
                    gc = hctx.guard()
                    def f(v, qc=qc, sub=sub, rot=rot, LASTPV=LASTPV, gc=gc):
                        v.wait_ge(S["PT"], LASTPV)
                        if gc:
                            v.wait_ge(S[gc[0]], gc[1])
                        v.reciprocal(s_rz[:], psC[sub][:, 128:129])
                        v.drain()
                        v.tensor_single_scalar(s_ctxf[:], psC[sub][:, 0:128],
                                               s_rz[:, 0:1], ALU.mult)
                        v.drain()
                        v.tensor_copy(s_cth[rot][:], s_ctxf[:])
                        v.drain()
                        v.scalar_tensor_tensor(s_ctl[rot][:], s_ctxf[:], 1.0,
                                               s_cth[rot][:], ALU.mult, ALU.subtract)
                        v.drain().then_inc(S["PV"], 1)
                    emit("v", f)
                    CTXV = inc("PV")
                    CTXV_prev_qc[sub] = CTXV

                    csl = "ctx0" if rot == 0 else "ctx1"
                    def f(g, qc=qc, sub=sub, rot=rot, h=h, CTXV=CTXV, csl=csl):
                        g.wait_ge(S["PV"], CTXV)
                        r0 = qc * 512 + sub * 128
                        g.dma_start(ctxh_in[r0:r0 + 128, h * 128:(h + 1) * 128],
                                    s_cth[rot][:]).then_inc(S[csl], 16)
                        g.dma_start(ctxl_in[r0:r0 + 128, h * 128:(h + 1) * 128],
                                    s_ctl[rot][:]).then_inc(S[csl], 16)
                    emit("g", f)
                    CTXST = inc(csl, 32)
                    hctx.push(csl, CTXST)

                if h == 0:
                    if qc <= 11:
                        slot_quant_tile()
                        slot_quant_tile()
                    elif qc == 12:
                        assert state["quant_next"] == len(WT)
                        slot_t_ags()
            pa.close()

        CTX_ST_ALL0 = cnt["ctx0"]
        CTX_ST_ALL1 = cnt["ctx1"]

        p1.close()
        pq.close()
        barrier()

        CTXAG_DELAY = {}

        def f(g, CTX_ST_ALL0=CTX_ST_ALL0, CTX_ST_ALL1=CTX_ST_ALL1):
            g.wait_ge(S["ctx0"], CTX_ST_ALL0)
            g.wait_ge(S["ctx1"], CTX_ST_ALL1)
            if "pt" in CTXAG_DELAY:
                g.wait_ge(S["PT"], CTXAG_DELAY["pt"])
            g.collective_compute("AllGather", ALU.bypass, RG,
                                 [ctxh_in.ap().opt()], [ctxh.ap().opt()]).then_inc(S["ccch"])
            g.collective_compute("AllGather", ALU.bypass, RG,
                                 [ctxl_in.ap().opt()], [ctxl.ap().opt()]).then_inc(S["cccl"])
        emit("g", f)
        CC_CTH = inc("ccch")
        CC_CTL = inc("cccl")

        def f(g, CC_CTH=CC_CTH, CC_CTL=CC_CTL):
            g.wait_ge(S["ccch"], CC_CTH)
            g.wait_ge(S["cccl"], CC_CTL)
            for r in range(8):
                g.reg_mov(tmpr, r * N * 256)
                g.reg_add(tmpr, tmpr, pidoff)
                g.dma_start(myctxh[0:TOK, r * 256:(r + 1) * 256],
                            bass.AP(ctxh, tmpr, [[256, TOK], [1, 256]])).then_inc(S["grp"], 16)
                g.reg_mov(tmpr, r * N * 256)
                g.reg_add(tmpr, tmpr, pidoff)
                g.dma_start(myctxl[0:TOK, r * 256:(r + 1) * 256],
                            bass.AP(ctxl, tmpr, [[256, TOK], [1, 256]])).then_inc(S["grp"], 16)
        emit("g", f)
        REPACK = inc("grp", 16 * 16)

        p2 = ExitStack()
        s_fxnh = p2.enter_context(nc.sbuf_tensor("s_fxnh", [128, 16 * 512], BF16,
                                                 is_transpose_dest=True))
        s_fxnl = p2.enter_context(nc.sbuf_tensor("s_fxnl", [128, 16 * 512], BF16,
                                                 is_transpose_dest=True))
        s_hidh = p2.enter_context(nc.sbuf_tensor("s_hidh", [128, 32 * 512], BF16))
        s_hidl = p2.enter_context(nc.sbuf_tensor("s_hidl", [128, 32 * 512], BF16))
        s_yacc = p2.enter_context(nc.sbuf_tensor("s_yacc", [128, 4 * D], F32))
        s_tu2 = [p2.enter_context(nc.sbuf_tensor(f"s_tu2_{i}", [128, 16 * 256], BF16,
                                                 is_transpose_dest=True)) for i in range(3)]
        s_tg2 = [p2.enter_context(nc.sbuf_tensor(f"s_tg2_{i}", [128, 16 * 256], BF16,
                                                 is_transpose_dest=True)) for i in range(3)]
        s_td2 = [p2.enter_context(nc.sbuf_tensor(f"s_td2_{i}", [128, 512], BF16,
                                                 is_transpose_dest=True)) for i in range(6)]
        s_silu = p2.enter_context(nc.sbuf_tensor("s_silu", [128, 512], F32))
        s_hidf = p2.enter_context(nc.sbuf_tensor("s_hidf", [128, 512], F32))
        s_outt = [p2.enter_context(nc.sbuf_tensor(f"s_outt{i}", [128, 512], F32))
                  for i in range(2)]
        psU = [p2.enter_context(nc.psum_tensor(f"psU{i}", [128, 512], F32)) for i in range(2)]
        psG2 = [p2.enter_context(nc.psum_tensor(f"psG2_{i}", [128, 512], F32)) for i in range(2)]
        psD = [p2.enter_context(nc.psum_tensor(f"psD{i}", [128, 512], F32)) for i in range(4)]

        htu = Hist(3)
        htd = Hist(6)
        tu_i = [0]
        td_i = [0]
        hyf = Hist(2)
        UGT_last_sc = 0
        HIDV_hist = Hist(2)
        CC_TU = cnt["cctu"]; CC_TG = cnt["cctg"]; CC_TD = cnt["cctd"]
        TLD_all = {}
        FXN_INFO = {}

        def emit_tld(sc_, es_):
            ti = tu_i[0]
            tb = ti % 3
            tu_i[0] += 1
            gtu = htu.marks[ti - 3] if ti >= 3 else None
            def f(e_, es=es_, gtu=gtu, tb=tb):
                if gtu:
                    e_.wait_ge(S[gtu[0]], gtu[1])
                for dc in range(16):
                    sl = S[f"tu{tb}{dc // 4}"]
                    e_.dma_start(s_tu2[tb][:, dc * 256:(dc + 1) * 256],
                                 tu[es * 256:(es + 1) * 256,
                                    dc * 128:(dc + 1) * 128],
                                 transpose=True).then_inc(sl, 16)
                    e_.dma_start(s_tg2[tb][:, dc * 256:(dc + 1) * 256],
                                 tgg[es * 256:(es + 1) * 256,
                                     dc * 128:(dc + 1) * 128],
                                 transpose=True).then_inc(sl, 16)
            emit("y", f)
            info = []
            for g in range(4):
                sl = f"tu{tb}{g}"
                info.append((sl, inc(sl, 16 * 8)))
            TLD_all[(sc_, es_)] = (tb, info)

        FX_SLOTS = ("yfx", "yfxb", "yfxc", "yfxd")

        def emit_fxn(sc_, guard_):
            info = []
            for g in range(4):
                def f(y, sc=sc_, g=g, guard_=guard_):
                    if g == 0:
                        if guard_ is None:
                            y.wait_ge(S["cctu"], CC_TU)
                            y.wait_ge(S["cctg"], CC_TG)
                            y.wait_ge(S["cctd"], CC_TD)
                        else:
                            y.wait_ge(S["PT"], guard_)
                    for dc in range(g * 4, g * 4 + 4):
                        y.dma_start(s_fxnh[:, dc * 512:(dc + 1) * 512],
                                    xnh_in[sc * 512:(sc + 1) * 512,
                                           dc * 128:(dc + 1) * 128],
                                    transpose=True).then_inc(S[FX_SLOTS[g]], 16)
                        y.dma_start(s_fxnl[:, dc * 512:(dc + 1) * 512],
                                    xnl_in[sc * 512:(sc + 1) * 512,
                                           dc * 128:(dc + 1) * 128],
                                    transpose=True).then_inc(S[FX_SLOTS[g]], 16)
                emit("y", f)
                info.append((FX_SLOTS[g], inc(FX_SLOTS[g], 16 * 8)))
            FXN_INFO[sc_] = info

        for sc in range(2):
            if sc == 0:
                emit_fxn(0, None)
            elif 1 not in FXN_INFO:
                emit_fxn(1, UGT_last_sc)

            DOWN_prev = 0
            for eh in range(2):
                for et in range(32):
                    eg = eh * 32 + et
                    es = eg // 2
                    if eg % 2 == 0 and (sc, es) not in TLD_all:
                        emit_tld(sc, es)
                    TLD = TLD_all[(sc, es)]

                    ghid = HIDV_hist.guard()
                    def f(t, sc=sc, eg=eg, es=es, et=et, FXN=FXN_INFO[sc],
                          TLD=TLD, ghid=ghid):
                        if ghid:
                            t.wait_ge(S[ghid[0]], ghid[1])
                        el = (eg % 2) * 128
                        bU = eg % 2
                        tb = TLD[0]
                        for dc in range(16):
                            if dc % 4 == 0:
                                g = dc // 4
                                if eg == 0:
                                    t.wait_ge(S[FXN[g][0]], FXN[g][1])
                                if eg % 2 == 0:
                                    t.wait_ge(S[TLD[1][g][0]], TLD[1][g][1])
                            lu = s_tu2[tb][:, dc * 256 + el: dc * 256 + el + 128]
                            lg = s_tg2[tb][:, dc * 256 + el: dc * 256 + el + 128]
                            rh = s_fxnh[:, dc * 512:(dc + 1) * 512]
                            rl = s_fxnl[:, dc * 512:(dc + 1) * 512]
                            t.matmul(psU[bU][:], lu, rh, start=(dc == 0), stop=False)
                            t.matmul(psU[bU][:], lu, rl, start=False, stop=(dc == 15))
                            t.matmul(psG2[bU][:], lg, rh, start=(dc == 0), stop=False)
                            mm = t.matmul(psG2[bU][:], lg, rl, start=False, stop=(dc == 15))
                        mm.then_inc(S["PT"], 1)
                    emit("t", f)
                    UGT = inc("PT")
                    if sc == 0 and eg == 31:
                        CTXAG_DELAY["pt"] = UGT
                    if eg == 63:
                        UGT_last_sc = UGT
                    if eg % 2 == 1:
                        htu.push("PT", UGT)

                    def f(s, eg=eg, UGT=UGT, THR=state["THR_BC"]):
                        s.wait_ge(S["PT"], UGT)
                        s.wait_ge(S["qbc"], THR)
                        s.activation(s_silu[:], psU[eg % 2][:], AF.Silu,
                                     scale=s_scales[:, 0:1]).then_inc(S["PS"], 1)
                    emit("s", f)
                    SIL = inc("PS")

                    def f(v, et=et, eg=eg, SIL=SIL):
                        v.wait_ge(S["PS"], SIL)
                        v.scalar_tensor_tensor(s_hidf[:], psG2[eg % 2][:],
                                               s_scales[:, 1:2], s_silu[:],
                                               ALU.mult, ALU.mult)
                        c0 = et * 512
                        v.drain()
                        v.tensor_copy(s_hidh[:, c0:c0 + 512], s_hidf[:])
                        v.drain()
                        v.scalar_tensor_tensor(s_hidl[:, c0:c0 + 512], s_hidf[:], 1.0,
                                               s_hidh[:, c0:c0 + 512],
                                               ALU.mult, ALU.subtract)
                        v.drain().then_inc(S["PV"], 1)
                    emit("v", f)
                    HIDV = inc("PV")
                    HIDV_hist.push("PV", HIDV)

                HID_HALF = cnt["PV"]

                for dc in range(4):
                    for ec in range(32):
                        if dc == 0 and ec == 6:
                            if eh == 0 and HOIST_EH in (1, 3):
                                for e2 in range(16, 19):
                                    if (sc, e2) not in TLD_all:
                                        emit_tld(sc, e2)
                            elif eh == 1 and sc == 0 and HOIST_EH in (2, 3):
                                emit_fxn(1, UGT_last_sc)
                                for e2 in range(3):
                                    if (1, e2) not in TLD_all:
                                        emit_tld(1, e2)
                        ege = eh * 32 + ec
                        db = td_i[0] % 6
                        td_i[0] += 1
                        gtd = htd.guard()
                        dsl = ("td0", "td1", "td2", "td3", "td4", "td5")[db]
                        def f(y, sc=sc, dc=dc, ec=ec, ege=ege, gtd=gtd, dsl=dsl, db=db):
                            if gtd:
                                y.wait_ge(S[gtd[0]], gtd[1])
                            for half in range(2):
                                y.dma_start(
                                    s_td2[db][:, half * 256:(half + 1) * 256],
                                    td[dc * 512 + half * 256:dc * 512 + (half + 1) * 256,
                                       ege * 128:(ege + 1) * 128],
                                    transpose=True).then_inc(S[dsl], 16)
                        emit("y", f)
                        DT_LD = (dsl, inc(dsl, 32), db)

                        def f(t, dc=dc, ec=ec, DT_LD=DT_LD, HID_HALF=HID_HALF,
                              DOWN_prev=DOWN_prev):
                            t.wait_ge(S[DT_LD[0]], DT_LD[1])
                            if ec == 0:
                                t.wait_ge(S["PV"], HID_HALF)
                                if DOWN_prev:
                                    t.wait_ge(S["PV"], DOWN_prev)
                            db = DT_LD[2]
                            for st in range(4):
                                lh = s_hidh[:, ec * 512 + st * 128:ec * 512 + (st + 1) * 128]
                                ll = s_hidl[:, ec * 512 + st * 128:ec * 512 + (st + 1) * 128]
                                t.matmul(psD[st][:], lh, s_td2[db][:],
                                         start=(ec == 0), stop=False)
                                mm = t.matmul(psD[st][:], ll, s_td2[db][:],
                                              start=False, stop=(ec == 31))
                            mm.then_inc(S["PT"], 1)
                        emit("t", f)
                        DMM = inc("PT")
                        htd.push("PT", DMM)

                    DMM_dc = cnt["PT"]
                    for st in range(4):
                        if eh == 0:
                            def f(v, sc=sc, dc=dc, st=st, DMM_dc=DMM_dc):
                                v.wait_ge(S["PT"], DMM_dc)
                                v.tensor_copy(
                                    s_yacc[:, st * D + dc * 512:st * D + (dc + 1) * 512],
                                    psD[st][:])
                                v.drain().then_inc(S["PV"], 1)
                            emit("v", f)
                            inc("PV")
                        else:
                            gyf = hyf.guard()
                            def f(v, sc=sc, dc=dc, st=st, DMM_dc=DMM_dc, gyf=gyf):
                                v.wait_ge(S["PT"], DMM_dc)
                                if gyf:
                                    v.wait_ge(S[gyf[0]], gyf[1])
                                v.tensor_add(s_outt[st % 2][:], psD[st][:],
                                             s_yacc[:, st * D + dc * 512:
                                                    st * D + (dc + 1) * 512])
                                v.drain().then_inc(S["PV"], 1)
                            emit("v", f)
                            YEVv = inc("PV")

                            ysl = "yf0" if st % 2 == 0 else "yf1"
                            def f(g, sc=sc, dc=dc, st=st, YEVv=YEVv, ysl=ysl):
                                g.wait_ge(S["PV"], YEVv)
                                r0 = sc * 512 + st * 128
                                g.dma_start(yff[r0:r0 + 128, dc * 512:(dc + 1) * 512],
                                            s_outt[st % 2][:]).then_inc(S[ysl], 16)
                            emit("g", f)
                            YST = inc(ysl, 16)
                            hyf.push(ysl, YST)

                    DOWN_prev = cnt["PV"]

        YFF_ST0 = cnt["yf0"]
        YFF_ST1 = cnt["yf1"]
        p2.close()
        barrier()

        po = ExitStack()
        s_ctTh = po.enter_context(nc.sbuf_tensor("s_ctTh", [128, 16 * TOK], BF16,
                                                 is_transpose_dest=True))
        s_ctTl = po.enter_context(nc.sbuf_tensor("s_ctTl", [128, 16 * TOK], BF16,
                                                 is_transpose_dest=True))
        s_woTh = [po.enter_context(nc.sbuf_tensor(f"s_woTh{i}", [128, 16 * 512], BF16,
                                                  is_transpose_dest=True)) for i in range(2)]
        s_woTl = [po.enter_context(nc.sbuf_tensor(f"s_woTl{i}", [128, 16 * 512], BF16,
                                                  is_transpose_dest=True)) for i in range(2)]
        s_xr = [po.enter_context(nc.sbuf_tensor(f"s_xr{i}", [128, 512], F32))
                for i in range(2)]
        s_yf = [po.enter_context(nc.sbuf_tensor(f"s_yf{i}", [128, 512], F32))
                for i in range(2)]
        s_b1 = po.enter_context(nc.sbuf_tensor("s_b1", [128, 512], F32))
        s_oE = [po.enter_context(nc.sbuf_tensor(f"s_oE{i}", [128, 512], F32))
                for i in range(2)]
        psO = [po.enter_context(nc.psum_tensor(f"psO{i}", [128, 512], F32))
               for i in range(2)]

        CT_SLOTS = ("yct", "yctb", "yctc", "yctd")

        def f(y, REPACK=REPACK):
            y.wait_ge(S["grp"], REPACK)
            for ic in range(16):
                sl = S[CT_SLOTS[ic // 4]]
                y.dma_start(s_ctTh[:, ic * TOK:(ic + 1) * TOK],
                            myctxh[0:TOK, ic * 128:(ic + 1) * 128],
                            transpose=True).then_inc(sl, 16)
                y.dma_start(s_ctTl[:, ic * TOK:(ic + 1) * TOK],
                            myctxl[0:TOK, ic * 128:(ic + 1) * 128],
                            transpose=True).then_inc(sl, 16)
        emit("y", f)
        CT_INFO = [(CT_SLOTS[g], inc(CT_SLOTS[g], 16 * 8)) for g in range(4)]

        ho_ev = Hist(2)
        hxr = Hist(2)
        hot = Hist(2)
        OPROJ_prev = [0, 0]
        for oc in range(4):
            gdw = OPROJ_prev[oc % 2]
            def f(y, oc=oc, gdw=gdw, CC_WOH=CC_WOH, CC_WOL=CC_WOL):
                y.wait_ge(S["ccwh"], CC_WOH)
                y.wait_ge(S["ccwl"], CC_WOL)
                if gdw:
                    y.wait_ge(S["PT"], gdw)
                for ic in range(16):
                    sl = S[f"wo{oc % 2}{ic // 4}"]
                    y.dma_start(s_woTh[oc % 2][:, ic * 512:ic * 512 + 512],
                                woh[oc * 512:(oc + 1) * 512, ic * 128:(ic + 1) * 128],
                                transpose=True).then_inc(sl, 16)
                    y.dma_start(s_woTl[oc % 2][:, ic * 512:ic * 512 + 512],
                                wol[oc * 512:(oc + 1) * 512, ic * 128:(ic + 1) * 128],
                                transpose=True).then_inc(sl, 16)
            emit("y", f)
            WO_INFO = [(f"wo{oc % 2}{g}", inc(f"wo{oc % 2}{g}", 16 * 8))
                       for g in range(4)]

            for st in range(8):
                bO = st % 2
                gxr = hxr.guard()
                rsl = "xr0" if st % 2 == 0 else "xr1"
                def f(y, oc=oc, st=st, gxr=gxr, rsl=rsl, Y0=YFF_ST0, Y1=YFF_ST1):
                    y.wait_ge(S["yf0"], Y0)
                    y.wait_ge(S["yf1"], Y1)
                    if gxr:
                        y.wait_ge(S[gxr[0]], gxr[1])
                    y.dma_start(s_xr[st % 2][:],
                                xs[st * 128:(st + 1) * 128, oc * 512:(oc + 1) * 512]
                                ).then_inc(S[rsl], 16)
                    y.dma_start(s_yf[st % 2][:],
                                yff[st * 128:(st + 1) * 128, oc * 512:(oc + 1) * 512]
                                ).then_inc(S[rsl], 16)
                emit("y", f)
                RES_LD = (rsl, inc(rsl, 32))

                gev = ho_ev.guard()
                def f(t, oc=oc, st=st, bO=bO, WO=WO_INFO, CT=CT_INFO, gev=gev):
                    if gev:
                        t.wait_ge(S[gev[0]], gev[1])
                    for ic in range(16):
                        if ic % 4 == 0 and st == 0:
                            g = ic // 4
                            if oc == 0:
                                t.wait_ge(S[CT[g][0]], CT[g][1])
                            t.wait_ge(S[WO[g][0]], WO[g][1])
                        cth = s_ctTh[:, ic * TOK + st * 128: ic * TOK + (st + 1) * 128]
                        ctl = s_ctTl[:, ic * TOK + st * 128: ic * TOK + (st + 1) * 128]
                        wh = s_woTh[oc % 2][:, ic * 512:ic * 512 + 512]
                        wl = s_woTl[oc % 2][:, ic * 512:ic * 512 + 512]
                        t.matmul(psO[bO][:], cth, wh, start=(ic == 0), stop=False)
                        t.matmul(psO[bO][:], cth, wl, start=False, stop=False)
                        mm = t.matmul(psO[bO][:], ctl, wh, start=False, stop=(ic == 15))
                    mm.then_inc(S["PT"], 1)
                emit("t", f)
                OMM = inc("PT")
                if st == 7:
                    OPROJ_prev[oc % 2] = OMM

                got = hot.guard()
                def f(v, bO=bO, oc=oc, st=st, OMM=OMM, got=got, RES_LD=RES_LD):
                    v.wait_ge(S["PT"], OMM)
                    v.wait_ge(S[RES_LD[0]], RES_LD[1])
                    if got:
                        v.wait_ge(S[got[0]], got[1])
                    v.tensor_add(s_b1[:], psO[bO][:], s_bob[:, oc * 512:(oc + 1) * 512])
                    v.drain()
                    v.tensor_add(s_b1[:], s_b1[:], s_xr[st % 2][:])
                    v.drain()
                    v.scalar_tensor_tensor(s_oE[bO][:], s_yf[st % 2][:],
                                           s_scales[:, 2:3], s_b1[:],
                                           ALU.mult, ALU.add)
                    v.drain().then_inc(S["PV"], 1)
                emit("v", f)
                OEV = inc("PV")
                ho_ev.push("PV", OEV)
                hxr.push("PV", OEV)

                osl = "ou0" if bO == 0 else "ou1"
                def f(g, oc=oc, st=st, bO=bO, OEV=OEV, osl=osl):
                    g.wait_ge(S["PV"], OEV)
                    g.dma_start(out[st * 128:(st + 1) * 128, oc * 512:(oc + 1) * 512],
                                s_oE[bO][:]).then_inc(S[osl], 16)
                emit("g", f)
                OST = inc(osl, 16)
                hot.push(osl, OST)

        if DEBUG:
            F0 = cnt["ou0"]; F1 = cnt["ou1"]
            def f(g, F0=F0, F1=F1):
                g.wait_ge(S["ou0"], F0)
                g.wait_ge(S["ou1"], F1)
                g.dma_start(dbg["dbg_xnh"][:, :], xnh_in[:, :]).then_inc(S["gsc"], 16)
                g.dma_start(dbg["dbg_ctxh"][:, :], ctxh_in[:, :]).then_inc(S["gsc"], 16)
                g.dma_start(dbg["dbg_yff"][:, :], yff[:, :]).then_inc(S["gsc"], 16)
                g.dma_start(dbg["dbg_scp"][:, :], scp[:, :]).then_inc(S["gsc"], 16)
                g.dma_start(dbg["dbg_tu"][:, :], tu_in[:, :]).then_inc(S["gsc"], 16)
            emit("g", f)
            DBG_ST = inc("gsc", 16 * 5)
            def f(y, DBG_ST=DBG_ST):
                y.wait_ge(S["gsc"], DBG_ST)
            emit("y", f)

        FIN0 = cnt["ou0"]
        FIN1 = cnt["ou1"]

        def f(y, FIN0=FIN0, FIN1=FIN1):
            y.wait_ge(S["ou0"], FIN0)
            y.wait_ge(S["ou1"], FIN1)
        emit("y", f)

        po.close()

        with nc.Block() as block:
            @block.sync
            def _(y):
                for fn in ops["y"]:
                    fn(y)

            @block.tensor
            def _(t):
                for fn in ops["t"]:
                    fn(t)

            @block.vector
            def _(v):
                for fn in ops["v"]:
                    fn(v)

            @block.scalar
            def _(s):
                for fn in ops["s"]:
                    fn(s)

            @block.gpsimd
            def _(g):
                for fn in ops["g"]:
                    fn(g)

    n_dedup = dedupe_ldweights(nc)
    return nc


_NC_CACHE = None


def kernel(L0_attn_x, L0_attn_wq, L0_attn_bq, L0_attn_wk, L0_attn_bk,
           L0_attn_wv, L0_attn_bv, L0_attn_wo, L0_attn_bo,
           L0_ffn_W_up, L0_ffn_W_gate, L0_ffn_W_down):
    global _NC_CACHE
    if _NC_CACHE is None:
        _NC_CACHE = build_program()
    nc = _NC_CACHE

    x = np.ascontiguousarray(np.asarray(L0_attn_x)).reshape(N, D)
    wq_ = np.asarray(L0_attn_wq)
    wk_ = np.asarray(L0_attn_wk)
    wv_ = np.asarray(L0_attn_wv)
    wo_ = np.asarray(L0_attn_wo)
    bq_ = np.asarray(L0_attn_bq).reshape(D, 1)
    bk_ = np.asarray(L0_attn_bk).reshape(D, 1)
    bv_ = np.asarray(L0_attn_bv).reshape(D, 1)
    bo_ = np.ascontiguousarray(np.asarray(L0_attn_bo)).reshape(1, D)
    wu_ = np.asarray(L0_ffn_W_up)
    wg_ = np.asarray(L0_ffn_W_gate)
    wd_ = np.asarray(L0_ffn_W_down)

    in_maps = []
    for c in range(8):
        in_maps.append(dict(
            xs=np.ascontiguousarray(x[c * TOK:(c + 1) * TOK]),
            wq=np.ascontiguousarray(wq_[c * 256:(c + 1) * 256]),
            wk=np.ascontiguousarray(wk_[c * 256:(c + 1) * 256]),
            wv=np.ascontiguousarray(wv_[c * 256:(c + 1) * 256]),
            wo=np.ascontiguousarray(wo_[c * 256:(c + 1) * 256]),
            bq=np.ascontiguousarray(bq_[c * 256:(c + 1) * 256]),
            bk=np.ascontiguousarray(bk_[c * 256:(c + 1) * 256]),
            bv=np.ascontiguousarray(bv_[c * 256:(c + 1) * 256]),
            bo=bo_,
            wu=np.ascontiguousarray(wu_[c * 1024:(c + 1) * 1024]),
            wgt=np.ascontiguousarray(wg_[c * 1024:(c + 1) * 1024]),
            wd=np.ascontiguousarray(wd_[c * 256:(c + 1) * 256]),
        ))

    res = run_bass_kernel_spmd(nc, in_maps, core_ids=list(range(8)))
    outp = np.concatenate([res.results[c]["out"] for c in range(8)], axis=0)
    if DEBUG:
        kernel.last_results = res
    return outp.reshape(1, N, D)
